# version 04
# baseline (speedup 1.0000x reference)
import numpy as np
import contextlib
import concourse.bass as bass
import concourse.mybir as mybir
from concourse.bass_utils import run_bass_kernel_spmd

F32 = mybir.dt.float32
BF16 = mybir.dt.bfloat16
I32 = mybir.dt.int32
AF = mybir.ActivationFunctionType
ALU = mybir.AluOpType
AX = mybir.AxisListType

H = 1024
FC = 8
T = 4096
NCH = 32
TT = 512
NT = 8
NH, HD, NL = 16, 64, 64
I_FF = 1408
IC = 11
NEXP = 8

UPTO = 5
REFINE_FLAG = 1


class Emit:
    def __init__(self, nc):
        self.nc = nc
        self.cnt = {}
        self.sems = {}

    def sem(self, es, name):
        s = es.enter_context(self.nc.semaphore(name))
        self.sems[name] = s
        self.cnt[name] = 0
        return s

    def inc(self, inst, name, n=1):
        inst.then_inc(self.sems[name], n)
        self.cnt[name] += n
        return self.cnt[name]

    def dinc(self, inst, name):
        return self.inc(inst, name, 16)

    def mark(self, name):
        return self.cnt[name]

    def wait(self, eng, name, val=None):
        eng.wait_ge(self.sems[name], self.cnt[name] if val is None else val)


def build_nc():
    nc = bass.Bass("TRN2", target_bir_lowering=False)
    dt = nc.dram_tensor

    xT = dt("xT", [H, T], F32, kind="ExternalInput").ap()
    halfoff = dt("halfoff", [1, 1], I32, kind="ExternalInput").ap()
    basisT = dt("basisT", [64, H], F32, kind="ExternalInput").ap()
    qcT = dt("qcT", [64, 128], F32, kind="ExternalInput").ap()
    kcT = dt("kcT", [64, 128], F32, kind="ExternalInput").ap()
    vcT = dt("vcT", [64, 128], F32, kind="ExternalInput").ap()
    ocT = dt("ocT", [64, 128], F32, kind="ExternalInput").ap()
    scal = dt("scal", [1, 4], F32, kind="ExternalInput").ap()
    idx = dt("idx", [1, 64], I32, kind="ExternalInput").ap()
    WqT = dt("WqT", [H, H], F32, kind="ExternalInput").ap()
    WkT = dt("WkT", [H, H], F32, kind="ExternalInput").ap()
    WvT = dt("WvT", [H, H], F32, kind="ExternalInput").ap()
    WoT = dt("WoT", [H, H], F32, kind="ExternalInput").ap()
    bqp = dt("bqp", [128, FC], F32, kind="ExternalInput").ap()
    bkp = dt("bkp", [128, FC], F32, kind="ExternalInput").ap()
    bvp = dt("bvp", [128, FC], F32, kind="ExternalInput").ap()
    bop = dt("bop", [128, FC], F32, kind="ExternalInput").ap()
    gate_wT = dt("gate_wT", [H, NEXP], F32, kind="ExternalInput").ap()
    sgwT = dt("sgwT", [H, 1], F32, kind="ExternalInput").ap()
    w_gate = dt("w_gate", [NEXP, H, I_FF], F32, kind="ExternalInput").ap()
    w_up = dt("w_up", [NEXP, H, I_FF], F32, kind="ExternalInput").ap()
    w_down = dt("w_down", [NEXP, I_FF, H], F32, kind="ExternalInput").ap()
    sw_gate = dt("sw_gate", [H, I_FF], F32, kind="ExternalInput").ap()
    sw_up = dt("sw_up", [H, I_FF], F32, kind="ExternalInput").ap()
    sw_down = dt("sw_down", [I_FF, H], F32, kind="ExternalInput").ap()

    y = dt("y", [H, 2048], F32, kind="ExternalOutput").ap()

    x2f = dt("x2f", [128, FC, T], F32, kind="Internal").ap()
    x3f = dt("x3f", [128, FC, T], F32, kind="Internal").ap()
    x3nf = dt("x3nf", [128, FC, T], BF16, kind="Internal").ap()
    Kf = dt("Kf", [128, FC, T], BF16, kind="Internal").ap()
    Vf = dt("Vf", [128, FC, T], BF16, kind="Internal").ap()
    Qf = dt("Qf", [128, FC, T], BF16, kind="Internal").ap()


    es = contextlib.ExitStack()
    em = Emit(nc)
    with es:
        def sb(name, shp, d=F32):
            return es.enter_context(nc.sbuf_tensor(name, shp, d))

        idf = sb("idf", [128, 128], F32)
        id_bf = sb("id_bf", [128, 128], BF16)
        ones_c = sb("ones_c", [128, 2], F32)
        ones_f = sb("ones_f", [128, 128], F32)
        ones_cb = sb("ones_cb", [128, 2], BF16)
        ones64 = sb("ones64", [64, 64], BF16)
        sc_in = sb("sc_in", [1, 4], F32)
        scb = sb("scb", [128, 8], F32)
        dpow_bro = sb("dpow_bro", [128, 128], BF16)
        dpow_rev = sb("dpow_rev", [128, 1], F32)
        dt_bf = sb("dt_bf", [128, 128], BF16)
        idl_bf = sb("idl_bf", [128, 128], BF16)
        stat1 = sb("stat1", [128, 4], F32)
        iotat = sb("iotat", [128, 128], I32)
        hoff_sb = sb("hoff_sb", [1, 1], I32)

        ARX = sb("ARX", [128, FC * 2048], F32)
        ARN2 = sb("ARN2", [128, FC * 2048], BF16)
        ARH = sb("ARH", [128, 2 * 11264], BF16)
        HAR = sb("HAR", [128, IC * 2048], BF16)
        ROWS = sb("ROWS", [128, 2048], F32)
        small = sb("small", [128, 1024], BF16)
        comb = sb("comb", [16, 2048], BF16)
        oneh = sb("oneh", [16, 9 * 128], BF16)
        gw9 = sb("gw9", [128, FC, 9], BF16)
        tb = sb("tb", [128, 64], F32)
        tmp2 = sb("tmp2", [128, 512], F32)
        combb = sb("combb", [128, 512], BF16)
        bia = sb("bia", [128, 32], F32)

        ARXbf = ARX.ap().bitcast(BF16)
        qw_vc = ARXbf[:, 16384:17408].rearrange("p (c k) -> p c k", k=128)
        kw_vc = ARXbf[:, 17408:18432].rearrange("p (c k) -> p c k", k=128)
        vw_vc = ARXbf[:, 18432:19456].rearrange("p (c k) -> p c k", k=128)
        owT = ARXbf[:, 19456:20480]
        basT = ARXbf[0:64, 20480:21504]
        coefT = ARXbf[0:64, 21504:22016].rearrange("p (a k) -> p a k", k=128)
        wrow = ARX[:, 11264:12288]
        wrow2 = ARXbf[:, 24576:25600]

        PS = [es.enter_context(nc.psum_tensor(f"PS{i}", [128, 512], F32)) for i in range(8)]

        for s in ["s_m", "s_xa", "s_xb", "s_wa", "s_wb", "s_wd", "s_t", "s_v", "s_s", "s_g", "s_y"]:
            em.sem(es, s)

        block = es.enter_context(nc.Block())
        C = em.cnt

        @block.sync
        def _(sync):
            em.dinc(sync.dma_start(out=sc_in[:], in_=scal), "s_m")
            em.dinc(sync.dma_start(out=wrow[0:64, 0:H], in_=basisT), "s_m")
            for i, t in enumerate([qcT, kcT, vcT, ocT]):
                em.dinc(sync.dma_start(out=wrow[64:128, i * 128:(i + 1) * 128], in_=t), "s_m")
        p0_loads = em.mark("s_m")

        @block.gpsimd
        def _(gpsimd):
            em.inc(gpsimd.iota(iotat[:], [[1, 128]], base=0, channel_multiplier=-1), "s_g")
        p0_iota = em.mark("s_g")

        @block.vector
        def _(vector):
            em.inc(vector.memset(ones_c[:], 1.0), "s_v")
            em.inc(vector.memset(ones_f[:], 1.0), "s_v")
            em.inc(vector.memset(ones_cb[:], 1.0), "s_v")
            em.inc(vector.memset(ones64[:], 1.0), "s_v")
            em.wait(vector, "s_g", p0_iota)
            em.inc(vector.tensor_scalar(idf[:], iotat[:], 0.0, None, ALU.is_equal), "s_v")
            em.inc(vector.tensor_scalar(id_bf[:], iotat[:], 0.0, None, ALU.is_equal), "s_v")
            em.wait(vector, "s_m", p0_loads)
            em.inc(vector.tensor_copy(basT[:], wrow[0:64, 0:H]), "s_v")
            em.inc(vector.tensor_copy(coefT[:], wrow[64:128, 0:512].rearrange("p (a b) -> p a b", b=128)), "s_v")
        p0_v1 = em.mark("s_v")

        @block.scalar
        def _(scalar):
            em.wait(scalar, "s_m", p0_loads)
            em.wait(scalar, "s_s", em.inc(scalar.activation(ROWS[0:1, 0:1], sc_in[0:1, 0:1], AF.Sigmoid), "s_s"))
            em.inc(scalar.activation(ROWS[0:1, 1:2], ROWS[0:1, 0:1], AF.Ln), "s_s")
        p0_s1 = em.mark("s_s")

        @block.vector
        def _(vector):
            em.wait(vector, "s_s", p0_s1)
            em.wait(vector, "s_v", em.inc(vector.reciprocal(ROWS[0:1, 2:3], sc_in[0:1, 2:3]), "s_v"))
            em.inc(vector.tensor_scalar(ROWS[0:1, 2:3], ROWS[0:1, 2:3], -1.0, None, ALU.mult), "s_v")
            em.inc(vector.tensor_copy(ROWS[0:1, 3:4], sc_in[0:1, 1:2]), "s_v")
            em.inc(vector.memset(ROWS[0:1, 4:5], 1.0 / 1024.0), "s_v")
            em.inc(vector.tensor_scalar(ROWS[0:1, 5:6], ROWS[0:1, 1:2], 128.0, None, ALU.mult), "s_v")
        p0_v2 = em.mark("s_v")

        @block.scalar
        def _(scalar):
            em.wait(scalar, "s_v", p0_v2)
            em.inc(scalar.activation(ROWS[0:1, 5:6], ROWS[0:1, 5:6], AF.Exp), "s_s")
        p0_s2 = em.mark("s_s")

        @block.tensor
        def _(tensor):
            em.wait(tensor, "s_s", p0_s2)
            tensor.nop()
        @block.vector
        def _(vector):
            em.wait(vector, "s_s", p0_s2)
            em.inc(vector.tensor_copy(ROWS[0:1, 8:9], ROWS[0:1, 3:4]), "s_v")
            em.inc(vector.tensor_copy(ROWS[0:1, 9:10], ROWS[0:1, 1:2]), "s_v")
            em.inc(vector.tensor_copy(ROWS[0:1, 10:11], ROWS[0:1, 2:3]), "s_v")
            em.inc(vector.tensor_copy(ROWS[0:1, 11:12], ROWS[0:1, 5:6]), "s_v")
            em.inc(vector.tensor_copy(ROWS[0:1, 12:13], ROWS[0:1, 4:5]), "s_v")
            em.inc(vector.memset(ROWS[0:1, 13:14], 1e-5), "s_v")
        p0_v3 = em.mark("s_v")

        @block.tensor
        def _(tensor):
            em.wait(tensor, "s_v", p0_v3)
            em.inc(tensor.matmul(PS[0][:, 0:6], ones_f[0:1, 0:128], ROWS[0:1, 8:14], start=True, stop=True), "s_t")
        p0_t1 = em.mark("s_t")

        @block.vector
        def _(vector):
            em.wait(vector, "s_t", p0_t1)
            em.inc(vector.tensor_copy(scb[:, 0:6], PS[0][:, 0:6]), "s_v")
        p0_v4 = em.mark("s_v")

        @block.gpsimd
        def _(gpsimd):
            em.wait(gpsimd, "s_v", p0_v1)
            gpsimd.iota(iotat[:, 0:1], [[0, 1]], base=127, channel_multiplier=-1)
            em.inc(gpsimd.iota(iotat[:, 1:2], [[0, 1]], base=0, channel_multiplier=1), "s_g")
        p0_g2 = em.mark("s_g")

        @block.vector
        def _(vector):
            em.wait(vector, "s_g", p0_g2)
            em.wait(vector, "s_v", p0_v4)
            em.inc(vector.tensor_copy(ROWS[:, 16:17], iotat[:, 0:1]), "s_v")
            em.inc(vector.tensor_copy(ROWS[:, 17:18], iotat[:, 1:2]), "s_v")
        p0_v5 = em.mark("s_v")

        @block.scalar
        def _(scalar):
            em.wait(scalar, "s_v", p0_v5)
            scalar.activation(dpow_rev[:], ROWS[:, 16:17], AF.Exp, scale=scb[:, 1:2])
            em.inc(scalar.activation(ROWS[:, 18:19], ROWS[:, 17:18], AF.Exp, scale=scb[:, 1:2]), "s_s")
        p0_s3 = em.mark("s_s")

        @block.tensor
        def _(tensor):
            em.wait(tensor, "s_s", p0_s3)
            em.inc(tensor.transpose(PS[1][0:1, 0:128], ROWS[:, 18:19], idf[:]), "s_t")
        p0_t2 = em.mark("s_t")

        @block.vector
        def _(vector):
            em.wait(vector, "s_t", p0_t2)
            em.inc(vector.tensor_copy(ROWS[0:1, 128:256], PS[1][0:1, 0:128]), "s_v")
        p0_v6 = em.mark("s_v")

        @block.tensor
        def _(tensor):
            em.wait(tensor, "s_v", p0_v6)
            em.inc(tensor.matmul(PS[2][:, 0:128], ones_f[0:1, 0:128], ROWS[0:1, 128:256], start=True, stop=True), "s_t")
        p0_t3 = em.mark("s_t")

        @block.vector
        def _(vector):
            em.wait(vector, "s_t", p0_t3)
            em.inc(vector.tensor_copy(dpow_bro[:], PS[2][:, 0:128]), "s_v")
        p0_v7 = em.mark("s_v")

        @block.gpsimd
        def _(gpsimd):
            em.wait(gpsimd, "s_v", p0_v5)
            em.inc(gpsimd.iota(iotat[:], [[1, 128]], base=-1, channel_multiplier=-1), "s_g")
        p0_g3 = em.mark("s_g")

        @block.vector
        def _(vector):
            em.wait(vector, "s_g", p0_g3)
            em.wait(vector, "s_v", em.inc(vector.tensor_copy(wrow[:, 0:128], iotat[:]), "s_v"))
            em.inc(vector.tensor_scalar(wrow[:, 128:256], wrow[:, 0:128], 0.0, None, ALU.max), "s_v")
            em.inc(vector.tensor_scalar(wrow[:, 256:384], wrow[:, 0:128], 0.0, None, ALU.is_ge), "s_v")
        p0_v8 = em.mark("s_v")

        @block.scalar
        def _(scalar):
            em.wait(scalar, "s_v", p0_v8)
            em.inc(scalar.activation(wrow[:, 384:512], wrow[:, 128:256], AF.Exp, scale=scb[:, 1:2]), "s_s")
        p0_s4 = em.mark("s_s")

        @block.vector
        def _(vector):
            em.wait(vector, "s_s", p0_s4)
            em.inc(vector.tensor_tensor(dt_bf[:], wrow[:, 384:512], wrow[:, 256:384], ALU.mult), "s_v")
            em.inc(vector.tensor_scalar(idl_bf[:], idf[:], scb[:, 3:4], None, ALU.mult), "s_v")
        p0_v9 = em.mark("s_v")

        for wi, (dst, use_soft) in enumerate([(qw_vc, True), (kw_vc, True), (vw_vc, True), (owT, False)]):
            @block.tensor
            def _(tensor, wi=wi):
                em.wait(tensor, "s_v", p0_v1)
                em.wait(tensor, "s_s")
                em.wait(tensor, "s_v")
                for half, bank in enumerate([PS[3], PS[5]]):
                    em.inc(tensor.matmul(bank[:, 0:512], coefT[:, wi, :],
                                         basT[:, half * 512:(half + 1) * 512], start=True, stop=True), "s_t")
            t_mm = em.mark("s_t")

            if use_soft:
                @block.vector
                def _(vector):
                    em.wait(vector, "s_t", t_mm)
                    em.inc(vector.reduce_max(stat1[:, 0:1], PS[3].ap(), axis=AX.X), "s_v")
                    em.wait(vector, "s_v", em.inc(vector.reduce_max(stat1[:, 1:2], PS[5].ap(), axis=AX.X), "s_v"))
                    em.wait(vector, "s_v", em.inc(vector.tensor_tensor(stat1[:, 0:1], stat1[:, 0:1], stat1[:, 1:2], ALU.max), "s_v"))
                    em.inc(vector.tensor_scalar(stat1[:, 0:1], stat1[:, 0:1], -1.0, None, ALU.mult), "s_v")
                v_mx = em.mark("s_v")

                @block.scalar
                def _(scalar):
                    em.wait(scalar, "s_v", v_mx)
                    em.inc(scalar.activation(wrow[:, 0:512], PS[3].ap(), AF.Exp, bias=stat1[:, 0:1],
                                             accum_out=stat1[:, 1:2]), "s_s")
                    em.inc(scalar.activation(wrow[:, 512:1024], PS[5].ap(), AF.Exp, bias=stat1[:, 0:1],
                                             accum_out=stat1[:, 2:3]), "s_s")
                s_exp = em.mark("s_s")

                @block.vector
                def _(vector):
                    em.wait(vector, "s_s", s_exp)
                    em.wait(vector, "s_v", em.inc(vector.tensor_tensor(stat1[:, 1:2], stat1[:, 1:2], stat1[:, 2:3], ALU.add), "s_v"))
                    em.wait(vector, "s_v", em.inc(vector.reciprocal(stat1[:, 0:1], stat1[:, 1:2]), "s_v"))
                    em.inc(vector.tensor_scalar(wrow2[:, 0:H], wrow[:, 0:H], stat1[:, 0:1], None, ALU.mult), "s_v")
                v_sm = em.mark("s_v")
            else:
                @block.vector
                def _(vector):
                    em.wait(vector, "s_t", t_mm)
                    em.inc(vector.tensor_copy(wrow2[:, 0:512], PS[3].ap()), "s_v")
                    em.inc(vector.tensor_copy(wrow2[:, 512:1024], PS[5].ap()), "s_v")
                v_sm = em.mark("s_v")

            if dst is owT:
                @block.vector
                def _(vector):
                    em.wait(vector, "s_v", v_sm)
                    em.inc(vector.tensor_copy(owT[:], wrow2[:, 0:H]), "s_v")
            else:
                @block.tensor
                def _(tensor):
                    em.wait(tensor, "s_v", v_sm)
                    for fcx in range(FC):
                        em.inc(tensor.transpose(PS[4].ap().bitcast(BF16)[:, fcx * 128:(fcx + 1) * 128],
                                                wrow2[:, fcx * 128:(fcx + 1) * 128], id_bf[:]), "s_t")
                t_tr = em.mark("s_t")

                @block.vector
                def _(vector, dst=dst):
                    em.wait(vector, "s_t", t_tr)
                    em.inc(vector.tensor_copy(dst[:], PS[4].ap().bitcast(BF16)[:, 0:1024].rearrange(
                        "p (a b) -> p a b", b=128)), "s_v")

        p0_done_v = em.mark("s_v")
        p0_done_t = em.mark("s_t")

        for e in range(9):
            @block.gpsimd
            def _(gpsimd, e=e):
                if e == 0:
                    em.wait(gpsimd, "s_v", p0_v8)
                else:
                    em.wait(gpsimd, "s_v")
                em.inc(gpsimd.iota(iotat[0:16, 0:128], [[0, 128]], base=-e, channel_multiplier=1), "s_g")

            @block.vector
            def _(vector, e=e):
                em.wait(vector, "s_g")
                em.inc(vector.tensor_scalar(oneh[0:16, e * 128:(e + 1) * 128], iotat[0:16, 0:128],
                                            0.0, None, ALU.is_equal), "s_v")
        @block.vector
        def _(vector):
            em.inc(vector.memset(comb[0:16, :], 0.0), "s_v")
        @block.gpsimd
        def _(gpsimd):
            em.wait(gpsimd, "s_wa")
            em.dinc(gpsimd.dma_start(out=gw9[:, :, 0:8], in_=gate_wT.rearrange("(c p) e -> p c e", p=128),
                                     allow_slow_non_contiguous=True), "s_wa")
            em.dinc(gpsimd.dma_start(out=gw9[:, :, 8:9], in_=sgwT.rearrange("(c p) o -> p c o", p=128),
                                     allow_slow_non_contiguous=True), "s_wa")
        @block.sync
        def _(sync):
            em.wait(sync, "s_m")
            for i, bap in enumerate([bqp, bkp, bvp, bop]):
                em.dinc(sync.dma_start(out=bia[:, i * 8:(i + 1) * 8], in_=bap), "s_m")
            em.dinc(sync.dma_start(out=iotat[0:1, 0:64].bitcast(I32) if False else hoff_sb[0:1, 0:1], in_=halfoff), "s_m")
        p0b_gw = em.mark("s_wa")
        p0b_v = em.mark("s_v")

        def wslot_gu(k):
            return ARH[:, (k % 2) * 11264:(k % 2) * 11264 + 11264].rearrange("p (c i) -> p c i", c=FC)
        def wslot_dn(k):
            return ARH[:, (k % 2) * 11264:(k % 2) * 11264 + 11264].rearrange("p (i c) -> p i c", i=IC)
        def expert_mats(e):
            if e < 8:
                return (w_gate[e:e + 1].squeeze(0).rearrange("(c p) i -> p c i", p=128),
                        w_up[e:e + 1].squeeze(0).rearrange("(c p) i -> p c i", p=128),
                        w_down[e:e + 1].squeeze(0).rearrange("(i p) c -> p i c", p=128))
            return (sw_gate.rearrange("(c p) i -> p c i", p=128),
                    sw_up.rearrange("(c p) i -> p c i", p=128),
                    sw_down.rearrange("(i p) c -> p i c", p=128))
        def wslot_ab(k):
            return ARH[:, (k % 2) * 11264:(k % 2) * 11264 + 8192].rearrange("p (c o) -> p c o", c=FC)
        wstream = []
        for g in range(2):
            for e in range(9):
                gm, um, dm = expert_mats(e)
                wstream += [(gm, "gu"), (um, "gu"), (dm, "dn")]
        n_ffn1_mats = len(wstream)
        if UPTO >= 4:
            for Wt in [WkT, WvT, WqT, WoT]:
                wstream.append((Wt.rearrange("(c p) o -> p c o", p=128), "ab"))
        if UPTO >= 5:
            for e in range(9):
                gm, um, dm = expert_mats(e)
                wstream += [(gm, "gu"), (um, "gu"), (dm, "dn")]

        wmarks = {}
        def emit_wloads(lo, hi):
            @block.gpsimd
            def _(gpsimd):
                for idx in range(lo, hi):
                    em.wait(gpsimd, "s_wa")
                    if idx >= 2:
                        em.wait(gpsimd, "s_wb", idx - 1)
                    ap, kind = wstream[idx]
                    dst = {"gu": wslot_gu, "dn": wslot_dn, "ab": wslot_ab}[kind](idx)
                    em.dinc(gpsimd.dma_start(out=dst, in_=ap), "s_wa")
                    wmarks[idx] = em.mark("s_wa")
        lidx_ld = p3_g = 0
        if UPTO >= 4:
            @block.sync
            def _(sync):
                em.wait(sync, "s_m")
                em.wait(sync, "s_v", p0b_v)
                em.dinc(sync.dma_start(out=iotat[0:1, 0:64], in_=idx), "s_m")
            lidx_ld = em.mark("s_m")
            @block.gpsimd
            def _(gpsimd):
                em.wait(gpsimd, "s_v", p0b_v)
                em.inc(gpsimd.iota(iotat[:, 64:65], [[0, 1]], base=0, channel_multiplier=1), "s_g")
            p3_g = em.mark("s_g")
        if UPTO >= 3:
            emit_wloads(0, min(len(wstream), n_ffn1_mats + 5))

        xt_v = ARX.ap().rearrange("p (b c t) -> p b c t", b=2, c=FC)[:, :, :, 0:512]
        xt = ARX.ap().rearrange("p (b x) -> p b x", b=4)
        def xt_buf(j):
            return ARX.ap().rearrange("p (b c t) -> p b c t", b=4, c=FC)[:, j % 2, :, :]
        def x1n_buf(j):
            return ARN2.ap().rearrange("p (b c t) -> p b c t", b=4, c=FC)[:, j % 2, :, :]
        def xbf_buf(j):
            return ARN2.ap().rearrange("p (b c t) -> p b c t", b=4, c=FC)[:, 2, :, :]
        def sqb_buf(j):
            return ARN2.ap().rearrange("p (b c t) -> p b c t", b=4, c=FC)[:, 3, :, :]
        sm = small.ap()
        M_bf = sm[:, 0:128]
        q_bf = sm[:, 128:256]
        qd_bf = sm[:, 256:384]
        k_bf = sm[:, 384:512]
        vt_bf = sm[:, 512:640]
        sdt_bf = sm[:, 640:768]
        kd_bf = sm[:, 768:896]
        retr_bf = sm[:, 896:1024]

        @block.vector
        def _(vector):
            em.inc(vector.memset(M_bf, 0.0), "s_v")
        p1_m0 = em.mark("s_v")

        if UPTO >= 1:
            for j in range(NT):
                psem = "s_xa" if j % 2 == 0 else "s_xb"
                @block.sync
                def _(sync, j=j, psem=psem):
                    if j >= 2:
                        em.wait(sync, "s_xa" if j % 2 == 0 else "s_xb")
                    em.dinc(sync.dma_start(out=xt_buf(j), in_=xT.rearrange("(c p) t -> p c t", p=128)[:, :, j * 512:(j + 1) * 512]), psem)
                ld_mark = em.mark(psem)

                @block.vector
                def _(vector, j=j, psem=psem, ld_mark=ld_mark):
                    em.wait(vector, psem, ld_mark)
                    em.wait(vector, "s_t")
                    em.inc(vector.tensor_copy(xbf_buf(j), xt_buf(j)), "s_v")
                v_cast = em.mark("s_v")

                @block.scalar
                def _(scalar, j=j, psem=psem, ld_mark=ld_mark):
                    em.wait(scalar, psem, ld_mark)
                    em.wait(scalar, "s_t")
                    em.inc(scalar.activation(sqb_buf(j), xt_buf(j), AF.Square), "s_s")
                s_sq = em.mark("s_s")

                @block.tensor
                def _(tensor, j=j, v_cast=v_cast, s_sq=s_sq):
                    em.wait(tensor, "s_v", v_cast)
                    em.wait(tensor, "s_s", s_sq)
                    for fcx in range(FC):
                        mm = tensor.matmul(PS[0][0:1, 0:512], ones_cb[:, 0:1], xbf_buf(j)[:, fcx, :],
                                           start=(fcx == 0), stop=(fcx == FC - 1))
                        if fcx == FC - 1:
                            em.inc(mm, "s_t")
                    for fcx in range(FC):
                        mm = tensor.matmul(PS[1][0:1, 0:512], ones_cb[:, 0:1], sqb_buf(j)[:, fcx, :],
                                           start=(fcx == 0), stop=(fcx == FC - 1))
                        if fcx == FC - 1:
                            em.inc(mm, "s_t")
                t_stat = em.mark("s_t")

                @block.vector
                def _(vector, t_stat=t_stat):
                    em.wait(vector, "s_t", t_stat)
                    r = ROWS
                    em.inc(vector.tensor_scalar(r[0:1, 256:768], PS[0][0:1, 0:512], -1.0 / 1024.0, None, ALU.mult), "s_v")
                    em.wait(vector, "s_v", em.inc(vector.tensor_scalar(r[0:1, 768:1280], PS[1][0:1, 0:512], 1.0 / 1024.0, None, ALU.mult), "s_v"))
                    em.wait(vector, "s_v", em.inc(vector.tensor_tensor(r[0:1, 1280:1792], r[0:1, 256:768], r[0:1, 256:768], ALU.mult), "s_v"))
                    em.inc(vector.tensor_tensor(r[0:1, 768:1280], r[0:1, 768:1280], r[0:1, 1280:1792], ALU.subtract), "s_v")
                v_var = em.mark("s_v")

                @block.scalar
                def _(scalar, v_var=v_var):
                    em.wait(scalar, "s_v", v_var)
                    em.inc(scalar.activation(ROWS[0:1, 1280:1792], ROWS[0:1, 768:1280], AF.Sqrt, bias=scb[0:1, 5:6]), "s_s")
                s_std = em.mark("s_s")

                @block.vector
                def _(vector, s_std=s_std):
                    em.wait(vector, "s_s", s_std)
                    em.wait(vector, "s_v", em.inc(vector.reciprocal(ROWS[0:1, 768:1280], ROWS[0:1, 1280:1792]), "s_v"))
                    em.inc(vector.tensor_tensor(ROWS[0:1, 1280:1792], ROWS[0:1, 256:768], ROWS[0:1, 768:1280], ALU.mult), "s_v")
                v_rstd = em.mark("s_v")

                @block.tensor
                def _(tensor, v_rstd=v_rstd):
                    em.wait(tensor, "s_v", v_rstd)
                    em.inc(tensor.matmul(PS[0][:, 0:512], ones_f[0:1, 0:128], ROWS[0:1, 768:1280], start=True, stop=True), "s_t")
                    em.inc(tensor.matmul(PS[1][:, 0:512], ones_f[0:1, 0:128], ROWS[0:1, 1280:1792], start=True, stop=True), "s_t")
                t_bro = em.mark("s_t")

                @block.vector
                def _(vector, j=j, t_bro=t_bro):
                    em.wait(vector, "s_t", t_bro)
                    for half in range(2):
                        em.wait(vector, "s_v")
                        for fcx in range(half * 4, half * 4 + 4):
                            em.inc(vector.tensor_tensor(ROWS[:, (fcx % 4) * 512:(fcx % 4 + 1) * 512],
                                                        xt_buf(j)[:, fcx, :], PS[0][:, 0:512], ALU.mult), "s_v")
                        em.wait(vector, "s_v")
                        for fcx in range(half * 4, half * 4 + 4):
                            em.inc(vector.tensor_tensor(x1n_buf(j)[:, fcx, :], ROWS[:, (fcx % 4) * 512:(fcx % 4 + 1) * 512],
                                                        PS[1][:, 0:512], ALU.add), "s_v")
                v_ln = em.mark("s_v")

                for cc in range(4):
                    ch = j * 4 + cc
                    tsl = slice(cc * 128, (cc + 1) * 128)
                    @block.tensor
                    def _(tensor, j=j, tsl=tsl, v_ln=v_ln):
                        em.wait(tensor, "s_v", em.mark("s_v"))
                        for fcx in range(FC):
                            mm = tensor.matmul(PS[0][:, 0:128], qw_vc[:, fcx, :], x1n_buf(j)[:, fcx, tsl],
                                               start=(fcx == 0), stop=(fcx == FC - 1))
                            if fcx == FC - 1:
                                em.inc(mm, "s_t")
                        for fcx in range(FC):
                            mm = tensor.matmul(PS[1][:, 0:128], kw_vc[:, fcx, :], x1n_buf(j)[:, fcx, tsl],
                                               start=(fcx == 0), stop=(fcx == FC - 1))
                            if fcx == FC - 1:
                                em.inc(mm, "s_t")
                        for fcx in range(FC):
                            mm = tensor.matmul(PS[2][:, 0:128], x1n_buf(j)[:, fcx, tsl], vw_vc[:, fcx, :],
                                               start=(fcx == 0), stop=(fcx == FC - 1))
                            if fcx == FC - 1:
                                em.inc(mm, "s_t")
                    t_qkv = em.mark("s_t")

                    @block.vector
                    def _(vector, t_qkv=t_qkv):
                        em.wait(vector, "s_t", t_qkv)
                        em.inc(vector.tensor_copy(q_bf, PS[0][:, 0:128]), "s_v")
                        em.inc(vector.tensor_tensor(qd_bf, PS[0][:, 0:128], dpow_bro[:], ALU.mult), "s_v")
                        em.inc(vector.tensor_copy(k_bf, PS[1][:, 0:128]), "s_v")
                        em.inc(vector.tensor_copy(vt_bf, PS[2][:, 0:128]), "s_v")
                    v_qkv = em.mark("s_v")

                    @block.tensor
                    def _(tensor, v_qkv=v_qkv):
                        em.wait(tensor, "s_v", v_qkv)
                        em.inc(tensor.matmul(PS[3][:, 0:128], k_bf, q_bf, start=True, stop=True), "s_t")
                        em.inc(tensor.transpose(PS[4].ap().bitcast(BF16)[:, 0:128], k_bf, id_bf[:]), "s_t")
                    t_st = em.mark("s_t")

                    @block.vector
                    def _(vector, t_st=t_st):
                        em.wait(vector, "s_t", t_st)
                        em.inc(vector.tensor_tensor(sdt_bf, PS[3][:, 0:128], dt_bf[:], ALU.mult), "s_v")
                        em.inc(vector.tensor_scalar(kd_bf, PS[4].ap().bitcast(BF16)[:, 0:128], dpow_rev[:], None, ALU.mult), "s_v")
                    v_sdt = em.mark("s_v")

                    @block.tensor
                    def _(tensor, v_sdt=v_sdt):
                        em.wait(tensor, "s_v", v_sdt)
                        tensor.matmul(PS[5][:, 0:128], vt_bf, sdt_bf, start=True, stop=False)
                        em.inc(tensor.matmul(PS[5][:, 0:128], M_bf, qd_bf, start=False, stop=True), "s_t")
                        tensor.matmul(PS[6][:, 0:128], idl_bf[:], M_bf, start=True, stop=False)
                        em.inc(tensor.matmul(PS[6][:, 0:128], kd_bf, vt_bf, start=False, stop=True), "s_t")
                    t_ret = em.mark("s_t")

                    @block.vector
                    def _(vector, t_ret=t_ret):
                        em.wait(vector, "s_t", t_ret)
                        em.inc(vector.tensor_copy(retr_bf, PS[5][:, 0:128]), "s_v")
                        em.inc(vector.tensor_copy(M_bf, PS[6][:, 0:128]), "s_v")
                    v_ret = em.mark("s_v")

                    @block.tensor
                    def _(tensor, v_ret=v_ret):
                        em.wait(tensor, "s_v", v_ret)
                        for vc in range(FC):
                            b = PS[7] if vc < 4 else PS[3]
                            em.inc(tensor.matmul(b[:, (vc % 4) * 128:(vc % 4 + 1) * 128], owT[:, vc * 128:(vc + 1) * 128],
                                                 retr_bf, start=True, stop=True), "s_t")
                    t_out = em.mark("s_t")

                    @block.vector
                    def _(vector, j=j, tsl=tsl, t_out=t_out):
                        em.wait(vector, "s_t", t_out)
                        for vc in range(FC):
                            b = PS[7] if vc < 4 else PS[3]
                            em.inc(vector.scalar_tensor_tensor(xt_buf(j)[:, vc, tsl], b[:, (vc % 4) * 128:(vc % 4 + 1) * 128],
                                                               scb[:, 0:1], xt_buf(j)[:, vc, tsl], ALU.mult, ALU.add), "s_v")

                v_tile_done = em.mark("s_v")
                @block.sync
                def _(sync, j=j, psem=psem, v_tile_done=v_tile_done):
                    em.wait(sync, "s_v", v_tile_done)
                    em.dinc(sync.dma_start(out=x2f[:, :, j * 512:(j + 1) * 512], in_=xt_buf(j)), psem)

        acc3 = ARX.ap().rearrange("p (c t) -> p c t", c=FC)
        xn3 = ARN2.ap().rearrange("p (c t) -> p c t", c=FC)
        h3 = HAR.ap().rearrange("p (i t) -> p i t", i=IC)
        hbf = HAR[:, 0:4096].rearrange("p (c t) -> p c t", c=FC)
        sqb2 = HAR[:, 4096:8192].rearrange("p (c t) -> p c t", c=FC)
        wcons = [0]

        def emit_ln2(src, dst):
            @block.vector
            def _(vector):
                em.wait(vector, "s_t")
                em.wait(vector, "s_m")
                em.inc(vector.tensor_copy(hbf[:, :, 0:512], src), "s_v")
            v_c = em.mark("s_v")

            @block.scalar
            def _(scalar):
                em.wait(scalar, "s_t")
                em.wait(scalar, "s_m")
                em.wait(scalar, "s_v", v_c)
                em.inc(scalar.activation(sqb2[:, :, 0:512], src, AF.Square), "s_s")
            s_q = em.mark("s_s")

            @block.tensor
            def _(tensor):
                em.wait(tensor, "s_v")
                em.wait(tensor, "s_s", s_q)
                for fcx in range(FC):
                    mm = tensor.matmul(PS[0][0:1, 0:512], ones_cb[:, 0:1], hbf[:, fcx, 0:512],
                                       start=(fcx == 0), stop=(fcx == FC - 1))
                    if fcx == FC - 1:
                        em.inc(mm, "s_t")
                for fcx in range(FC):
                    mm = tensor.matmul(PS[1][0:1, 0:512], ones_cb[:, 0:1], sqb2[:, fcx, 0:512],
                                       start=(fcx == 0), stop=(fcx == FC - 1))
                    if fcx == FC - 1:
                        em.inc(mm, "s_t")
            t_s = em.mark("s_t")

            @block.vector
            def _(vector):
                em.wait(vector, "s_t", t_s)
                r = ROWS
                em.inc(vector.tensor_scalar(r[0:1, 256:768], PS[0][0:1, 0:512], -1.0 / 1024.0, None, ALU.mult), "s_v")
                em.wait(vector, "s_v", em.inc(vector.tensor_scalar(r[0:1, 768:1280], PS[1][0:1, 0:512], 1.0 / 1024.0, None, ALU.mult), "s_v"))
                em.wait(vector, "s_v", em.inc(vector.tensor_tensor(r[0:1, 1280:1792], r[0:1, 256:768], r[0:1, 256:768], ALU.mult), "s_v"))
                em.inc(vector.tensor_tensor(r[0:1, 768:1280], r[0:1, 768:1280], r[0:1, 1280:1792], ALU.subtract), "s_v")
            v_v = em.mark("s_v")

            @block.scalar
            def _(scalar):
                em.wait(scalar, "s_v", v_v)
                em.inc(scalar.activation(ROWS[0:1, 1280:1792], ROWS[0:1, 768:1280], AF.Sqrt, bias=scb[0:1, 5:6]), "s_s")
            s_r = em.mark("s_s")

            @block.vector
            def _(vector):
                em.wait(vector, "s_s", s_r)
                em.wait(vector, "s_v", em.inc(vector.reciprocal(ROWS[0:1, 768:1280], ROWS[0:1, 1280:1792]), "s_v"))
                em.inc(vector.tensor_tensor(ROWS[0:1, 1280:1792], ROWS[0:1, 256:768], ROWS[0:1, 768:1280], ALU.mult), "s_v")
            v_r = em.mark("s_v")

            @block.tensor
            def _(tensor):
                em.wait(tensor, "s_v", v_r)
                em.inc(tensor.matmul(PS[0][:, 0:512], ones_f[0:1, 0:128], ROWS[0:1, 768:1280], start=True, stop=True), "s_t")
                em.inc(tensor.matmul(PS[1][:, 0:512], ones_f[0:1, 0:128], ROWS[0:1, 1280:1792], start=True, stop=True), "s_t")
            t_b = em.mark("s_t")

            @block.vector
            def _(vector):
                em.wait(vector, "s_t", t_b)
                for half in range(2):
                    em.wait(vector, "s_v")
                    for fcx in range(half * 4, half * 4 + 4):
                        em.inc(vector.tensor_tensor(ROWS[:, (fcx % 4) * 512:(fcx % 4 + 1) * 512],
                                                    src[:, fcx, :], PS[0][:, 0:512], ALU.mult), "s_v")
                    em.wait(vector, "s_v")
                    for fcx in range(half * 4, half * 4 + 4):
                        em.inc(vector.tensor_tensor(dst[:, fcx, :], ROWS[:, (fcx % 4) * 512:(fcx % 4 + 1) * 512],
                                                    PS[1][:, 0:512], ALU.add), "s_v")
            return em.mark("s_v")

        def emit_router(sub):
            ssl = slice(sub * 512, (sub + 1) * 512)
            @block.tensor
            def _(tensor):
                em.wait(tensor, "s_wa", p0b_gw)
                em.wait(tensor, "s_v")
                for fcx in range(FC):
                    mm = tensor.matmul(PS[6][0:9, 0:512], gw9[:, fcx, :], xn3[:, fcx, ssl],
                                       start=(fcx == 0), stop=(fcx == FC - 1))
                    if fcx == FC - 1:
                        em.inc(mm, "s_t")
            t_l = em.mark("s_t")
            @block.vector
            def _(vector):
                em.wait(vector, "s_t", t_l)
                em.inc(vector.tensor_copy(ROWS[0:9, 0:512], PS[6][0:9, 0:512]), "s_v")
            for q in range(4):
                qsl = slice(sub * 512 + q * 128, sub * 512 + (q + 1) * 128)
                @block.tensor
                def _(tensor, qsl=qsl):
                    em.wait(tensor, "s_v")
                    em.wait(tensor, "s_s")
                    em.inc(tensor.transpose(PS[7][:, 0:9], ROWS[0:9, (qsl.start - sub * 512):(qsl.start - sub * 512) + 128], idf[0:9, 0:9]), "s_t")
                t_t = em.mark("s_t")
                @block.vector
                def _(vector, t_t=t_t):
                    em.wait(vector, "s_t", t_t)
                    l8 = PS[7][:, 0:8]
                    em.inc(vector.reduce_max(tb[:, 0:1], l8, axis=AX.X), "s_v")
                    em.wait(vector, "s_v", em.inc(vector.reduce_max(tb[:, 1:2], l8, axis=AX.X, negate=True), "s_v"))
                    em.wait(vector, "s_v", em.inc(vector.tensor_scalar(tb[:, 8:16], l8, tb[:, 0:1], None, ALU.is_equal), "s_v"))
                    em.wait(vector, "s_v", em.inc(vector.scalar_tensor_tensor(tb[:, 16:24], tb[:, 8:16], -1e30, l8, ALU.mult, ALU.add), "s_v"))
                    em.wait(vector, "s_v", em.inc(vector.reduce_max(tb[:, 2:3], tb[:, 16:24], axis=AX.X), "s_v"))
                    em.inc(vector.tensor_scalar(tb[:, 24:32], l8, tb[:, 2:3], None, ALU.is_ge), "s_v")
                v_t2 = em.mark("s_v")
                @block.scalar
                def _(scalar, v_t2=v_t2):
                    em.wait(scalar, "s_v", v_t2)
                    em.inc(scalar.activation(tb[:, 32:40], PS[7][:, 0:8], AF.Exp, bias=tb[:, 1:2]), "s_s")
                    em.inc(scalar.activation(tb[:, 56:57], PS[7][:, 8:9], AF.Sigmoid), "s_s")
                s_e = em.mark("s_s")
                @block.vector
                def _(vector, s_e=s_e):
                    em.wait(vector, "s_s", s_e)
                    em.wait(vector, "s_v", em.inc(vector.tensor_tensor(tb[:, 40:48], tb[:, 32:40], tb[:, 24:32], ALU.mult), "s_v"))
                    em.wait(vector, "s_v", em.inc(vector.reduce_sum(tb[:, 3:4], tb[:, 40:48], axis=AX.X), "s_v"))
                    em.wait(vector, "s_v", em.inc(vector.reciprocal(tb[:, 4:5], tb[:, 3:4]), "s_v"))
                    em.inc(vector.tensor_scalar(tb[:, 48:56], tb[:, 40:48], tb[:, 4:5], None, ALU.mult), "s_v")
                v_cb = em.mark("s_v")
                @block.tensor
                def _(tensor, q=q, sub=sub, v_cb=v_cb):
                    em.wait(tensor, "s_v", v_cb)
                    em.wait(tensor, "s_s")
                    em.inc(tensor.transpose(PS[6][0:9, q * 128:(q + 1) * 128], tb[:, 48:57], idf[:]), "s_t")
                t_tb = em.mark("s_t")
                @block.vector
                def _(vector, q=q, qsl=qsl, t_tb=t_tb):
                    em.wait(vector, "s_t", t_tb)
                    em.inc(vector.tensor_copy(comb[0:9, qsl], PS[6][0:9, q * 128:(q + 1) * 128]), "s_v")

        def emit_expert(e):
            gi = wcons[0]
            smarks = []
            for sub in range(4):
                for ic in range(IC):
                    n = sub * IC + ic
                    last_mat = (n == 4 * IC - 1)
                    bank = PS[n % 2]
                    @block.tensor
                    def _(tensor, n=n, ic=ic, sub=sub, bank=bank, gi=gi, last_mat=last_mat):
                        if n == 0:
                            em.wait(tensor, "s_wa", wmarks[gi])
                            em.wait(tensor, "s_v")
                            em.wait(tensor, "s_s")
                        if n >= 2:
                            em.wait(tensor, "s_s", smarks[n - 2])
                        for fcx in range(FC):
                            mm = tensor.matmul(bank[:, 0:512], wslot_gu(gi)[:, fcx, ic * 128:(ic + 1) * 128],
                                               xn3[:, fcx, sub * 512:(sub + 1) * 512],
                                               start=(fcx == 0), stop=(fcx == FC - 1))
                            if fcx == FC - 1:
                                em.inc(mm, "s_wb" if last_mat else "s_t")
                    t_g = em.mark("s_t")
                    wb_g = em.mark("s_wb")
                    @block.scalar
                    def _(scalar, ic=ic, sub=sub, bank=bank, t_g=t_g, wb_g=wb_g, last_mat=last_mat):
                        if last_mat:
                            em.wait(scalar, "s_wb", wb_g)
                        else:
                            em.wait(scalar, "s_t", t_g)
                        em.inc(scalar.activation(h3[:, ic, sub * 512:(sub + 1) * 512], bank[:, 0:512], AF.Silu), "s_s")
                    smarks.append(em.mark("s_s"))
            wcons[0] += 1
            ui = wcons[0]
            vmarks = []
            for sub in range(4):
                for ic in range(IC):
                    n = sub * IC + ic
                    last_mat = (n == 4 * IC - 1)
                    bank = PS[2 + n % 2]
                    @block.tensor
                    def _(tensor, n=n, ic=ic, sub=sub, bank=bank, ui=ui, last_mat=last_mat):
                        if n == 0:
                            em.wait(tensor, "s_wa", wmarks[ui])
                        if n >= 2:
                            em.wait(tensor, "s_v", vmarks[n - 2])
                        for fcx in range(FC):
                            mm = tensor.matmul(bank[:, 0:512], wslot_gu(ui)[:, fcx, ic * 128:(ic + 1) * 128],
                                               xn3[:, fcx, sub * 512:(sub + 1) * 512],
                                               start=(fcx == 0), stop=(fcx == FC - 1))
                            if fcx == FC - 1:
                                em.inc(mm, "s_wb" if last_mat else "s_t")
                    t_u = em.mark("s_t")
                    wb_u = em.mark("s_wb")
                    @block.vector
                    def _(vector, ic=ic, sub=sub, bank=bank, t_u=t_u, wb_u=wb_u, last_mat=last_mat):
                        if last_mat:
                            em.wait(vector, "s_wb", wb_u)
                        else:
                            em.wait(vector, "s_t", t_u)
                        em.wait(vector, "s_s")
                        em.inc(vector.tensor_tensor(h3[:, ic, sub * 512:(sub + 1) * 512],
                                                    h3[:, ic, sub * 512:(sub + 1) * 512], bank[:, 0:512], ALU.mult), "s_v")
                    vmarks.append(em.mark("s_v"))
            wcons[0] += 1
            di = wcons[0]
            emarks = []
            for sub in range(4):
                @block.tensor
                def _(tensor, sub=sub, e=e, di=di):
                    if sub == 0:
                        em.wait(tensor, "s_wa", wmarks[di])
                    em.wait(tensor, "s_v")
                    em.inc(tensor.matmul(PS[6][:, 0:512], oneh[0:16, e * 128:(e + 1) * 128],
                                         comb[0:16, sub * 512:(sub + 1) * 512], start=True, stop=True), "s_t")
                t_cb = em.mark("s_t")
                @block.vector
                def _(vector, t_cb=t_cb):
                    em.wait(vector, "s_t", t_cb)
                    em.inc(vector.tensor_copy(combb[:], PS[6][:, 0:512]), "s_v")
                for hc in range(FC):
                    n = sub * FC + hc
                    last_mat = (n == 4 * FC - 1)
                    bank = PS[4 + n % 2]
                    @block.tensor
                    def _(tensor, n=n, hc=hc, sub=sub, bank=bank, di=di, last_mat=last_mat):
                        if n >= 2:
                            em.wait(tensor, "s_v", emarks[n - 2])
                        for ic in range(IC):
                            mm = tensor.matmul(bank[:, 0:512], wslot_dn(di)[:, ic, hc * 128:(hc + 1) * 128],
                                               h3[:, ic, sub * 512:(sub + 1) * 512],
                                               start=(ic == 0), stop=(ic == IC - 1))
                            if ic == IC - 1:
                                em.inc(mm, "s_wb" if last_mat else "s_t")
                    t_d = em.mark("s_t")
                    wb_d = em.mark("s_wb")
                    @block.vector
                    def _(vector, hc=hc, sub=sub, bank=bank, t_d=t_d, wb_d=wb_d, last_mat=last_mat):
                        if last_mat:
                            em.wait(vector, "s_wb", wb_d)
                        else:
                            em.wait(vector, "s_t", t_d)
                        em.wait(vector, "s_v")
                        em.wait(vector, "s_v", em.inc(vector.tensor_tensor(tmp2[:], bank[:, 0:512], combb[:], ALU.mult), "s_v"))
                        em.inc(vector.tensor_tensor(acc3[:, hc, sub * 512:(sub + 1) * 512],
                                                    acc3[:, hc, sub * 512:(sub + 1) * 512], tmp2[:], ALU.add), "s_v")
                    emarks.append(em.mark("s_v"))
            wcons[0] += 1

        def emit_ffn_supertile(writeback_base=None):
            for sub in range(4):
                emit_ln2(acc3[:, :, sub * 512:(sub + 1) * 512], xn3[:, :, sub * 512:(sub + 1) * 512])
                emit_router(sub)
            for e in range(9):
                emit_expert(e)
            if writeback_base is not None:
                for sub in range(4):
                    emit_ln2(acc3[:, :, sub * 512:(sub + 1) * 512], xn3[:, :, sub * 512:(sub + 1) * 512])
                @block.sync
                def _(sync, g0=writeback_base):
                    em.wait(sync, "s_m")
                    em.wait(sync, "s_v")
                    em.dinc(sync.dma_start(out=x3f[:, :, g0:g0 + 2048], in_=acc3), "s_m")
                    em.dinc(sync.dma_start(out=x3nf[:, :, g0:g0 + 2048], in_=xn3), "s_m")

        if UPTO >= 3:
            for g in range(2):
                @block.sync
                def _(sync, g=g):
                    em.wait(sync, "s_m")
                    em.wait(sync, "s_xa")
                    em.wait(sync, "s_xb")
                    em.wait(sync, "s_v")
                    em.wait(sync, "s_s")
                    em.wait(sync, "s_t")
                    em.dinc(sync.dma_start(out=acc3, in_=x2f[:, :, g * 2048:(g + 1) * 2048]), "s_m")
                emit_ffn_supertile(writeback_base=g * 2048)

        if UPTO >= 4:
            HARf = HAR.ap().bitcast(F32)
            ARN2f = ARN2.ap().bitcast(F32)

            def xnb(k):
                return HAR[:, (k % 2) * 4096:(k % 2) * 4096 + 4096].rearrange("p (c t) -> p c t", c=FC)
            stg = [HAR[:, 8192 + i * 512:8192 + (i + 1) * 512] for i in range(2)]
            S_sb = HAR[:, 9216:11264].rearrange("p (c j) -> p c j", j=64)
            KT4 = HAR[:, 11264:11776].rearrange("p (q d) -> p q d", q=4)
            LmT_acc = HARf[0:64, 10240:11264].rearrange("p (c d) -> p c d", c=FC)
            Z_acc = HARf[0:64, 2048:3072].rearrange("p (h j) -> p h j", h=16)
            lidxb = HARf[:, 5888:5952]
            pc_col = HARf[:, 5952:5953]
            tcol_f = HARf[:, 5953:5954]
            Y_bf = HAR[0:64, 21408:22432].rearrange("p (h d) -> p h d", h=16)
            w_bf = HAR[0:64, 22432:22448]
            Lm_all = HAR[0:64, 13312:14336].rearrange("p (h j) -> p h j", h=16)
            Lm2_all = HAR[0:64, 14336:15360].rearrange("p (h j) -> p h j", h=16)
            LmT_sb = HAR[:, 16672:17696].rearrange("p (c d) -> p c d", c=FC)
            Ln_sb = HARf[0:64, 7680:7696]
            Ksq_sb = HAR[:, 15392:15904]
            Phi_sb = HAR[0:64, 15904:16416]
            Phi_s2 = HAR[0:64, 6144:6656]
            PhiT_s2 = HAR[:, 6656:6912].rearrange("p (q j) -> p q j", q=4)
            PhiT_sb = HAR[:, 16416:16672].rearrange("p (q j) -> p q j", q=4)
            VT2 = HAR[:, 16672:20768].rearrange("p (q c d) -> p q c d", q=4, c=FC)
            Klo = HAR[0:64, 12288:12800]
            Ksqlo = HAR[0:64, 12800:13312]
            ideps = HARf[0:64, 8080:8144]
            twoI8 = HARf[0:64, 8144:8656]
            W_all = HARf[0:64, 8656:9680].rearrange("p (h j) -> p h j", h=16)
            X_all = HARf[0:64, 9680:10704].rearrange("p (h j) -> p h j", h=16)
            T2_sb = HARf[0:64, 5888:6400]
            Knb_B = HARf[0:64, 10384:10896]
            Knb_C = HARf[0:64, 4104:4616]
            rnb_sb = HARf[0:64, 4616:5128]
            ZU_all = ARN2f[0:64, 0:1040].rearrange("p (h j) -> p h j", h=16)
            Yu_all = ARN2f[0:64, 1040:2080].rearrange("p (h j) -> p h j", h=16)
            AO_fm = HAR[:, 0:4096].rearrange("p (c t) -> p c t", c=FC)
            Q_fm = ARN2.ap().rearrange("p (c t) -> p c t", c=FC)

            @block.vector
            def _(vector):
                em.wait(vector, "s_m", lidx_ld)
                em.wait(vector, "s_g", p3_g)
                em.wait(vector, "s_v")
                em.wait(vector, "s_v", em.inc(vector.tensor_copy(ROWS[0:1, 0:64], iotat[0:1, 0:64]), "s_v"))
                em.wait(vector, "s_v", em.inc(vector.tensor_scalar(ROWS[0:1, 0:64], ROWS[0:1, 0:64], 0.0, 4095.0, ALU.max, ALU.min), "s_v"))
                em.inc(vector.tensor_copy(tcol_f[:], iotat[:, 64:65]), "s_v")
            p3_v1 = em.mark("s_v")
            @block.tensor
            def _(tensor):
                em.wait(tensor, "s_v", p3_v1)
                em.inc(tensor.matmul(PS[6][:, 0:64], ones_f[0:1, 0:128], ROWS[0:1, 0:64], start=True, stop=True), "s_t")
            p3_t1 = em.mark("s_t")
            @block.vector
            def _(vector):
                em.wait(vector, "s_t", p3_t1)
                em.wait(vector, "s_v", em.inc(vector.tensor_copy(lidxb[:], PS[6][:, 0:64]), "s_v"))
                for c in range(32):
                    em.wait(vector, "s_v", em.inc(vector.tensor_scalar(pc_col[:], tcol_f[:], float(c * 128), None, ALU.add), "s_v"))
                    em.wait(vector, "s_v", em.inc(vector.tensor_scalar(S_sb[:, c, :], lidxb[:], pc_col[:], None, ALU.is_equal), "s_v"))

            stg2 = HAR[:, 12288:20480].rearrange("p (b c t) -> p b c t", b=2, c=FC)

            def emit_proj_pass(widx, bia_c, dst_dram, gather):
                st_marks = []
                ev_marks = []
                for tt in range(NT):
                    @block.sync
                    def _(sync, tt=tt):
                        em.wait(sync, "s_m")
                        em.wait(sync, "s_v")
                        em.dinc(sync.dma_start(out=xnb(tt), in_=x3nf[:, :, tt * 512:(tt + 1) * 512]), "s_m")
                    ld_m = em.mark("s_m")
                    for oc in range(FC):
                        n = tt * FC + oc
                        last_mat = (n == NT * FC - 1)
                        bank = PS[n % 4]
                        @block.tensor
                        def _(tensor, tt=tt, oc=oc, n=n, bank=bank, widx=widx, ld_m=ld_m, last_mat=last_mat):
                            em.wait(tensor, "s_m", ld_m)
                            if n == 0:
                                em.wait(tensor, "s_wa", wmarks[widx])
                                em.wait(tensor, "s_v")
                                em.wait(tensor, "s_s")
                            if n >= 4:
                                em.wait(tensor, "s_v", ev_marks[n - 4])
                            for fcx in range(FC):
                                mm = tensor.matmul(bank[:, 0:512], wslot_ab(widx)[:, fcx, oc * 128:(oc + 1) * 128],
                                                   xnb(tt)[:, fcx, :], start=(fcx == 0), stop=(fcx == FC - 1))
                                if fcx == FC - 1:
                                    em.inc(mm, "s_wb" if last_mat else "s_t")
                        t_p = em.mark("s_t")
                        wb_p = em.mark("s_wb")
                        @block.vector
                        def _(vector, tt=tt, oc=oc, n=n, bank=bank, bia_c=bia_c, t_p=t_p, wb_p=wb_p, last_mat=last_mat):
                            if last_mat:
                                em.wait(vector, "s_wb", wb_p)
                            else:
                                em.wait(vector, "s_t", t_p)
                            if oc == 0 and len(st_marks) >= 2:
                                em.wait(vector, "s_m", st_marks[-2])
                            em.inc(vector.tensor_scalar(stg2[:, tt % 2, oc, :], bank[:, 0:512],
                                                        bia[:, bia_c + oc:bia_c + oc + 1], None, ALU.add), "s_v")
                        ev_marks.append(em.mark("s_v"))
                        if gather:
                            v_e = ev_marks[-1]
                            for q in range(4):
                                @block.tensor
                                def _(tensor, tt=tt, oc=oc, q=q, v_e=v_e):
                                    em.wait(tensor, "s_v", v_e)
                                    em.wait(tensor, "s_v")
                                    em.inc(tensor.transpose(PS[6].ap().bitcast(BF16)[:, 0:128],
                                                            stg2[:, tt % 2, oc, q * 128:(q + 1) * 128], id_bf[:]), "s_t")
                                t_tr = em.mark("s_t")
                                @block.vector
                                def _(vector, q=q, t_tr=t_tr):
                                    em.wait(vector, "s_t", t_tr)
                                    em.inc(vector.tensor_copy(KT4[:, q, :], PS[6].ap().bitcast(BF16)[:, 0:128]), "s_v")
                            v_kt = em.mark("s_v")
                            @block.tensor
                            def _(tensor, tt=tt, oc=oc, v_kt=v_kt):
                                em.wait(tensor, "s_v", v_kt)
                                for q in range(4):
                                    em.inc(tensor.matmul(PS[5][0:64, 0:128], S_sb[:, tt * 4 + q, :], KT4[:, q, :],
                                                         start=(q == 0), stop=(q == 3)), "s_t")
                            t_ga = em.mark("s_t")
                            @block.vector
                            def _(vector, oc=oc, t_ga=t_ga):
                                em.wait(vector, "s_t", t_ga)
                                em.wait(vector, "s_v")
                                em.inc(vector.tensor_tensor(LmT_acc[:, oc, :], LmT_acc[:, oc, :], PS[5][0:64, 0:128],
                                                            ALU.add), "s_v")
                            ev_marks[-1] = em.mark("s_v")
                    @block.sync
                    def _(sync, tt=tt):
                        em.wait(sync, "s_m")
                        em.wait(sync, "s_v")
                        em.dinc(sync.dma_start(out=dst_dram[:, :, tt * 512:(tt + 1) * 512], in_=stg2[:, tt % 2, :, :]), "s_m")
                    st_marks.append(em.mark("s_m"))

            wk_i = n_ffn1_mats
            emit_proj_pass(wk_i, 8, Kf, gather=True)
            emit_proj_pass(wk_i + 1, 16, Vf, gather=False)
            emit_proj_pass(wk_i + 2, 0, Qf, gather=False)
            wcons[0] = wk_i + 3

            for oc in range(FC):
                @block.tensor
                def _(tensor, oc=oc):
                    em.wait(tensor, "s_v")
                    em.inc(tensor.transpose(PS[6][:, 0:64], LmT_acc[:, oc, :], idf[0:64, 0:64]), "s_t")
                t_lm = em.mark("s_t")
                @block.vector
                def _(vector, oc=oc, t_lm=t_lm):
                    em.wait(vector, "s_t", t_lm)
                    em.wait(vector, "s_v", em.inc(vector.tensor_copy(Lm_all[:, 2 * oc, :], PS[6][0:64, 0:64]), "s_v"))
                    em.inc(vector.tensor_copy(Lm_all[:, 2 * oc + 1, :], PS[6][64:128, 0:64]), "s_v")
            p3_lm0 = em.mark("s_v")
            @block.vector
            def _(vector):
                em.wait(vector, "s_v", p3_lm0)
                em.inc(vector.tensor_scalar(Lm2_all.rearrange("p h j -> p (h j)"), Lm_all.rearrange("p h j -> p (h j)"),
                                            -2.0, None, ALU.mult), "s_v")
            p3_lm = em.mark("s_v")

            for h in range(NH):
                @block.scalar
                def _(scalar, h=h):
                    em.wait(scalar, "s_v", p3_lm)
                    em.wait(scalar, "s_t")
                    em.inc(scalar.activation(ROWS[0:64, 0:64], Lm_all[:, h, :], AF.Square), "s_s")
                s_lsq = em.mark("s_s")
                @block.tensor
                def _(tensor, h=h, s_lsq=s_lsq):
                    em.wait(tensor, "s_s", s_lsq)
                    em.wait(tensor, "s_v")
                    em.inc(tensor.matmul(PS[6][0:1, 0:64], ones_f[0:64, 0:1], ROWS[0:64, 0:64],
                                         start=True, stop=True), "s_t")
                t_ln = em.mark("s_t")
                @block.vector
                def _(vector, t_ln=t_ln):
                    em.wait(vector, "s_t", t_ln)
                    em.inc(vector.tensor_copy(ROWS[0:1, 64:128], PS[6][0:1, 0:64]), "s_v")
                v_lnr = em.mark("s_v")
                @block.tensor
                def _(tensor, v_lnr=v_lnr):
                    em.wait(tensor, "s_v", v_lnr)
                    em.inc(tensor.transpose(PS[6][0:64, 64:65], ROWS[0:1, 64:128], idf[0:1, 0:1]), "s_t")
                t_lnc = em.mark("s_t")
                @block.vector
                def _(vector, h=h, t_lnc=t_lnc):
                    em.wait(vector, "s_t", t_lnc)
                    em.inc(vector.tensor_copy(Ln_sb[:, h:h + 1], PS[6][0:64, 64:65]), "s_v")

            Vtile = HAR[:, 8192:12288].rearrange("p (c t) -> p c t", c=FC)
            @block.vector
            def _(vector):
                em.wait(vector, "s_t")
                em.inc(vector.memset(ROWS[0:64, 1024:1040], 0.0), "s_v")
                em.inc(vector.memset(Z_acc.rearrange("p h j -> p (h j)"), 0.0), "s_v")
            Kt_B = HAR[:, 0:4096].rearrange("p (c t) -> p c t", c=FC)
            ts_marks = {}
            phi_marks = {}
            pt_marks = {}
            z_marks = []
            ksq_marks = {}
            for tt in range(NT):
                @block.sync
                def _(sync, tt=tt):
                    em.wait(sync, "s_m")
                    em.wait(sync, "s_v")
                    em.wait(sync, "s_s")
                    em.wait(sync, "s_t")
                    em.dinc(sync.dma_start(out=Kt_B, in_=Kf[:, :, tt * 512:(tt + 1) * 512]), "s_m")
                    em.dinc(sync.dma_start(out=Vtile.rearrange("p c t -> p (c t)"),
                                           in_=Vf[:, :, tt * 512:(tt + 1) * 512]), "s_m")
                ld_b = em.mark("s_m")
                vt_marks = []
                for oc in range(FC):
                    for q in range(4):
                        @block.tensor
                        def _(tensor, oc=oc, q=q, ld_b=ld_b):
                            em.wait(tensor, "s_m", ld_b)
                            if vt_marks:
                                em.wait(tensor, "s_v", vt_marks[-1])
                            elif z_marks:
                                em.wait(tensor, "s_v", z_marks[-1])
                            em.inc(tensor.transpose(PS[7].ap().bitcast(BF16)[:, 0:128],
                                                    Vtile[:, oc, q * 128:(q + 1) * 128], id_bf[:]), "s_t")
                        t_vt = em.mark("s_t")
                        @block.vector
                        def _(vector, oc=oc, q=q, t_vt=t_vt):
                            em.wait(vector, "s_t", t_vt)
                            em.inc(vector.tensor_copy(VT2[:, q, oc, :], PS[7].ap().bitcast(BF16)[:, 0:128]), "s_v")
                        vt_marks.append(em.mark("s_v"))
                for h in range(NH):
                    oc, par = h // 2, h % 2
                    Phi_b = Phi_sb if par == 0 else Phi_s2
                    PhiT_b = PhiT_sb if par == 0 else PhiT_s2
                    tsreg = ROWS[0:64, 512:1024] if par == 0 else ROWS[0:64, 0:512]
                    gh = tt * NH + h
                    if par == 0:
                        @block.scalar
                        def _(scalar, tt=tt, oc=oc, gh=gh):
                            em.wait(scalar, "s_m", ld_b)
                            if gh - 1 in ts_marks:
                                em.wait(scalar, "s_v", ts_marks[gh - 1])
                            em.inc(scalar.activation(Ksq_sb[:], Kt_B[:, oc, :], AF.Square), "s_s")
                        ksq_marks[gh] = em.mark("s_s")
                        @block.vector
                        def _(vector, gh=gh, oc=oc):
                            em.wait(vector, "s_s", ksq_marks[gh])
                            if gh - 1 in ts_marks:
                                em.wait(vector, "s_v", ts_marks[gh - 1])
                            em.inc(vector.tensor_copy(Klo[:], Kt_B[64:128, oc, :]), "s_v")
                            em.inc(vector.tensor_copy(Ksqlo[:], Ksq_sb[64:128, :]), "s_v")
                    s_ksq = ksq_marks[gh - par]
                    v_shift = em.mark("s_v")
                    rK = (lambda oc=oc: Kt_B[0:64, oc, :]) if par == 0 else (lambda: Klo[:])
                    rS = (lambda: Ksq_sb[0:64, :]) if par == 0 else (lambda: Ksqlo[:])
                    @block.tensor
                    def _(tensor, h=h, gh=gh, par=par, rK=rK, rS=rS, s_ksq=s_ksq, v_shift=v_shift):
                        em.wait(tensor, "s_s", s_ksq)
                        em.wait(tensor, "s_v", v_shift)
                        if gh - 2 in ts_marks:
                            em.wait(tensor, "s_v", ts_marks[gh - 2])
                        tensor.matmul(PS[par][0:64, 0:512], Lm2_all[:, h, :], rK(), start=True, stop=False)
                        em.inc(tensor.matmul(PS[par][0:64, 0:512], ones64[:], rS(), start=False, stop=True), "s_t")
                    t_xl = em.mark("s_t")
                    @block.vector
                    def _(vector, par=par, gh=gh, t_xl=t_xl, tsreg=tsreg):
                        em.wait(vector, "s_t", t_xl)
                        if gh - 2 in phi_marks:
                            em.wait(vector, "s_s", phi_marks[gh - 2])
                        em.inc(vector.tensor_scalar(tsreg, PS[par][0:64, 0:512],
                                                    Ln_sb[:, h:h + 1], 0.0, ALU.add, ALU.max), "s_v")
                    ts_marks[gh] = em.mark("s_v")
                    @block.scalar
                    def _(scalar, gh=gh, tsreg=tsreg, Phi_b=Phi_b):
                        em.wait(scalar, "s_v", ts_marks[gh])
                        if gh - 2 in pt_marks:
                            em.wait(scalar, "s_t", pt_marks[gh - 2])
                        em.inc(scalar.activation(Phi_b[:], tsreg, AF.Exp, scale=scb[0:64, 2:3]), "s_s")
                    phi_marks[gh] = em.mark("s_s")
                    @block.vector
                    def _(vector, h=h, gh=gh, Phi_b=Phi_b):
                        em.wait(vector, "s_s", phi_marks[gh])
                        em.wait(vector, "s_v", em.inc(vector.reduce_sum(tb[0:64, 60:61], Phi_b[:], axis=AX.X), "s_v"))
                        em.inc(vector.tensor_tensor(ROWS[0:64, 1024 + h:1025 + h], ROWS[0:64, 1024 + h:1025 + h],
                                                    tb[0:64, 60:61], ALU.add), "s_v")
                    ptc = []
                    for q in range(4):
                        @block.tensor
                        def _(tensor, gh=gh, q=q, Phi_b=Phi_b):
                            em.wait(tensor, "s_s", phi_marks[gh])
                            if ptc:
                                em.wait(tensor, "s_v", ptc[-1])
                            em.inc(tensor.transpose(PS[7].ap().bitcast(BF16)[:, 0:64],
                                                    Phi_b[:, q * 128:(q + 1) * 128], id_bf[0:64, 0:64]), "s_t")
                        t_pt = em.mark("s_t")
                        @block.vector
                        def _(vector, q=q, t_pt=t_pt, PhiT_b=PhiT_b):
                            em.wait(vector, "s_t", t_pt)
                            em.inc(vector.tensor_copy(PhiT_b[:, q, :], PS[7].ap().bitcast(BF16)[:, 0:64]), "s_v")
                        ptc.append(em.mark("s_v"))
                    pt_marks[gh] = em.mark("s_t")
                    v_pt = em.mark("s_v")
                    zbank = PS[4 + par]
                    @block.tensor
                    def _(tensor, h=h, oc=oc, par=par, v_pt=v_pt, PhiT_b=PhiT_b, zbank=zbank):
                        em.wait(tensor, "s_v", v_pt)
                        if len(z_marks) >= 2:
                            em.wait(tensor, "s_v", z_marks[-2])
                        for q in range(4):
                            em.inc(tensor.matmul(zbank[0:64, 0:64], PhiT_b[:, q, :],
                                                 VT2[:, q, oc, par * 64:(par + 1) * 64],
                                                 start=(q == 0), stop=(q == 3)), "s_t")
                    t_z = em.mark("s_t")
                    @block.vector
                    def _(vector, h=h, t_z=t_z, zbank=zbank):
                        em.wait(vector, "s_t", t_z)
                        em.inc(vector.tensor_tensor(Z_acc[:, h, :], Z_acc[:, h, :], zbank[0:64, 0:64], ALU.add), "s_v")
                    z_marks.append(em.mark("s_v"))

            @block.vector
            def _(vector):
                em.wait(vector, "s_t")
                em.inc(vector.tensor_scalar(ideps[:], idf[0:64, 0:64], 1e-6, None, ALU.mult), "s_v")
                for hh in range(8):
                    em.inc(vector.tensor_scalar(twoI8[:, hh * 64:(hh + 1) * 64], idf[0:64, 0:64], 2.0, None, ALU.mult), "s_v")
            p3_c = em.mark("s_v")
            for h in range(NH):
                oc, par = h // 2, h % 2
                p0 = 64 * par
                @block.tensor
                def _(tensor, h=h):
                    em.wait(tensor, "s_v")
                    em.inc(tensor.transpose(PS[6][0:1, 0:64], Ln_sb[:, h:h + 1], idf[0:64, 0:64]), "s_t")
                t_lr = em.mark("s_t")
                @block.vector
                def _(vector, t_lr=t_lr):
                    em.wait(vector, "s_t", t_lr)
                    em.inc(vector.tensor_copy(ROWS[0:1, 64:128], PS[6][0:1, 0:64]), "s_v")
                v_lr = em.mark("s_v")
                @block.tensor
                def _(tensor, v_lr=v_lr):
                    em.wait(tensor, "s_v", v_lr)
                    em.inc(tensor.matmul(PS[6][0:64, 128:192], ones_f[0:1, 0:64], ROWS[0:1, 64:128], start=True, stop=True), "s_t")
                t_lb = em.mark("s_t")
                @block.vector
                def _(vector, t_lb=t_lb):
                    em.wait(vector, "s_t", t_lb)
                    em.inc(vector.tensor_copy(Knb_B[:, 0:64], PS[6][0:64, 128:192]), "s_v")
                v_lb = em.mark("s_v")
                bank = PS[par]
                @block.tensor
                def _(tensor, h=h, bank=bank, v_lb=v_lb):
                    em.wait(tensor, "s_v", v_lb)
                    em.inc(tensor.matmul(bank[0:64, 0:64], Lm2_all[:, h, :], Lm_all[:, h, :],
                                         start=True, stop=True), "s_t")
                t_xlw = em.mark("s_t")
                @block.vector
                def _(vector, h=h, bank=bank, t_xlw=t_xlw):
                    em.wait(vector, "s_t", t_xlw)
                    em.wait(vector, "s_v", em.inc(vector.scalar_tensor_tensor(ROWS[0:64, 128:192], bank[0:64, 0:64],
                                                                              Ln_sb[:, h:h + 1], Knb_B[:, 0:64],
                                                                              ALU.add, ALU.add), "s_v"))
                    em.inc(vector.tensor_scalar(ROWS[0:64, 192:256], ROWS[0:64, 128:192], 0.0, scb[0:64, 2:3],
                                                ALU.max, ALU.mult), "s_v")
                v_wexp = em.mark("s_v")
                @block.scalar
                def _(scalar, v_wexp=v_wexp):
                    em.wait(scalar, "s_v", v_wexp)
                    em.inc(scalar.activation(ROWS[0:64, 128:192], ROWS[0:64, 192:256], AF.Exp), "s_s")
                s_wx = em.mark("s_s")
                @block.vector
                def _(vector, h=h, s_wx=s_wx):
                    em.wait(vector, "s_s", s_wx)
                    em.wait(vector, "s_v", p3_c)
                    em.inc(vector.tensor_tensor(W_all[:, h, :], ROWS[0:64, 128:192], ideps[:], ALU.add), "s_v")
            p3_w = em.mark("s_v")

            @block.vector
            def _(vector):
                em.wait(vector, "s_v", p3_w)
                em.inc(vector.tensor_scalar(X_all.rearrange("p h j -> p (h j)"), W_all.rearrange("p h j -> p (h j)"),
                                            1.0 / 4096.0, None, ALU.mult), "s_v")
            ns_init = em.mark("s_v")

            NS_IT = 26
            for it in range(NS_IT):
                for bh in range(2):
                    hs = range(bh * 8, bh * 8 + 8)
                    @block.tensor
                    def _(tensor, hs=hs):
                        em.wait(tensor, "s_v")
                        for h in hs:
                            em.inc(tensor.matmul(PS[bh][0:64, (h % 8) * 64:(h % 8 + 1) * 64],
                                                 W_all[:, h, :], X_all[:, h, :], start=True, stop=True), "s_t")
                    t_t1 = em.mark("s_t")
                    @block.vector
                    def _(vector, t_t1=t_t1):
                        em.wait(vector, "s_t", t_t1)
                        em.inc(vector.scalar_tensor_tensor(T2_sb[:], PS[bh][0:64, 0:512], -1.0, twoI8[:],
                                                           ALU.mult, ALU.add), "s_v")
                    v_t2 = em.mark("s_v")
                    @block.tensor
                    def _(tensor, hs=hs, v_t2=v_t2):
                        em.wait(tensor, "s_v", v_t2)
                        for h in hs:
                            em.inc(tensor.matmul(PS[2 + bh][0:64, (h % 8) * 64:(h % 8 + 1) * 64],
                                                 X_all[:, h, :], T2_sb[:, (h % 8) * 64:(h % 8 + 1) * 64],
                                                 start=True, stop=True), "s_t")
                    t_x2 = em.mark("s_t")
                    @block.vector
                    def _(vector, bh=bh, t_x2=t_x2):
                        em.wait(vector, "s_t", t_x2)
                        em.inc(vector.tensor_copy(X_all[:, bh * 8:(bh + 1) * 8, :].rearrange("p h j -> p (h j)"),
                                                  PS[2 + bh][0:64, 0:512]), "s_v")
            ns_done = em.mark("s_v")

            @block.vector
            def _(vector):
                em.wait(vector, "s_t")
                for h in range(NH):
                    em.inc(vector.tensor_copy(ZU_all[:, h, 0:64], Z_acc[:, h, :]), "s_v")
                    em.inc(vector.tensor_copy(ZU_all[:, h, 64:65], ROWS[0:64, 1024 + h:1025 + h]), "s_v")
            zu_v = em.mark("s_v")
            for h in range(NH):
                bank = PS[h % 2]
                @block.tensor
                def _(tensor, h=h, bank=bank):
                    em.wait(tensor, "s_v")
                    em.inc(tensor.matmul(bank[0:64, 0:65], X_all[:, h, :], ZU_all[:, h, 0:65], start=True, stop=True), "s_t")
                t_y0 = em.mark("s_t")
                @block.vector
                def _(vector, h=h, bank=bank, t_y0=t_y0):
                    em.wait(vector, "s_t", t_y0)
                    em.inc(vector.tensor_copy(Yu_all[:, h, 0:65], bank[0:64, 0:65]), "s_v")
                v_y0 = em.mark("s_v")
                if not REFINE_FLAG:
                    @block.vector
                    def _(vector, h=h, v_y0=v_y0):
                        em.wait(vector, "s_v", v_y0)
                        em.inc(vector.tensor_copy(Y_bf[:, h, :], Yu_all[:, h, 0:64]), "s_v")
                        em.inc(vector.tensor_copy(w_bf[:, h:h + 1], Yu_all[:, h, 64:65]), "s_v")
                    continue
                @block.tensor
                def _(tensor, h=h, bank=bank, v_y0=v_y0):
                    em.wait(tensor, "s_v", v_y0)
                    em.inc(tensor.matmul(bank[0:64, 128:193], W_all[:, h, :], Yu_all[:, h, 0:65], start=True, stop=True), "s_t")
                t_wy = em.mark("s_t")
                @block.vector
                def _(vector, h=h, bank=bank, t_wy=t_wy):
                    em.wait(vector, "s_t", t_wy)
                    em.inc(vector.scalar_tensor_tensor(ROWS[0:64, 0:65], bank[0:64, 128:193], -1.0,
                                                       ZU_all[:, h, 0:65], ALU.mult, ALU.add), "s_v")
                v_rr2 = em.mark("s_v")
                @block.tensor
                def _(tensor, h=h, bank=bank, v_rr2=v_rr2):
                    em.wait(tensor, "s_v", v_rr2)
                    em.inc(tensor.matmul(bank[0:64, 256:321], X_all[:, h, :], ROWS[0:64, 0:65], start=True, stop=True), "s_t")
                t_dy = em.mark("s_t")
                @block.vector
                def _(vector, h=h, bank=bank, t_dy=t_dy):
                    em.wait(vector, "s_t", t_dy)
                    em.wait(vector, "s_v", em.inc(vector.tensor_tensor(Yu_all[:, h, 0:65], Yu_all[:, h, 0:65],
                                                                       bank[0:64, 256:321], ALU.add), "s_v"))
                    em.inc(vector.tensor_copy(Y_bf[:, h, :], Yu_all[:, h, 0:64]), "s_v")
                    em.inc(vector.tensor_copy(w_bf[:, h:h + 1], Yu_all[:, h, 64:65]), "s_v")
            solve_done = em.mark("s_v")

            @block.gpsimd
            def _(gpsimd):
                em.wait(gpsimd, "s_m")
                em.wait(gpsimd, "s_v", solve_done)
                em.wait(gpsimd, "s_wa")
                pid = gpsimd.partition_id()
                off = gpsimd.compute_val((pid % 2) * 2048)
                em.dinc(gpsimd.dma_start(out=Q_fm.rearrange("p c t -> p (c t)"),
                                         in_=Qf[:, :, bass.DynSlice(off, 2048)]), "s_wa")
                em.dinc(gpsimd.dma_start(out=acc3.rearrange("p c t -> p (c t)"),
                                         in_=x3f[:, :, bass.DynSlice(off, 2048)]), "s_wa")
            q_ld = em.mark("s_wa")
            if len(wstream) > n_ffn1_mats + 5:
                emit_wloads(n_ffn1_mats + 5, len(wstream))
            for ct in range(4):
                tsl = slice(ct * 512, (ct + 1) * 512)
                for h in range(NH):
                    oc, par = h // 2, h % 2
                    if par == 0:
                        @block.scalar
                        def _(scalar, ct=ct, oc=oc):
                            em.wait(scalar, "s_wa", q_ld)
                            em.wait(scalar, "s_t")
                            em.inc(scalar.activation(Ksq_sb[:], Q_fm[:, oc, tsl], AF.Square), "s_s")
                        s_qsq0 = em.mark("s_s")
                        @block.vector
                        def _(vector, ct=ct, oc=oc, s_qsq0=s_qsq0):
                            em.wait(vector, "s_s", s_qsq0)
                            em.wait(vector, "s_t")
                            em.inc(vector.tensor_copy(Klo[:], Q_fm[64:128, oc, tsl]), "s_v")
                            em.inc(vector.tensor_copy(Ksqlo[:], Ksq_sb[64:128, :]), "s_v")
                    s_qsq = em.mark("s_s")
                    rQ = (lambda oc=oc, tsl=tsl: Q_fm[0:64, oc, tsl]) if par == 0 else (lambda: Klo[:])
                    rSq = (lambda: Ksq_sb[0:64, :]) if par == 0 else (lambda: Ksqlo[:])
                    @block.tensor
                    def _(tensor, h=h, par=par, rQ=rQ, rSq=rSq, s_qsq=s_qsq):
                        em.wait(tensor, "s_wa", q_ld)
                        em.wait(tensor, "s_s", s_qsq)
                        em.wait(tensor, "s_v")
                        tensor.matmul(PS[par][0:64, 0:512], Lm2_all[:, h, :], rQ(), start=True, stop=False)
                        em.inc(tensor.matmul(PS[par][0:64, 0:512], ones64[:], rSq(), start=False, stop=True), "s_t")
                    t_xq = em.mark("s_t")
                    @block.vector
                    def _(vector, h=h, par=par, t_xq=t_xq):
                        em.wait(vector, "s_t", t_xq)
                        em.inc(vector.tensor_scalar(ROWS[0:64, 512:1024], PS[par][0:64, 0:512],
                                                    Ln_sb[:, h:h + 1], 0.0, ALU.add, ALU.max), "s_v")
                    v_phq = em.mark("s_v")
                    @block.scalar
                    def _(scalar, v_phq=v_phq):
                        em.wait(scalar, "s_v", v_phq)
                        em.wait(scalar, "s_t")
                        em.inc(scalar.activation(Phi_sb[:], ROWS[0:64, 512:1024], AF.Exp, scale=scb[0:64, 2:3]), "s_s")
                    s_phq = em.mark("s_s")
                    @block.tensor
                    def _(tensor, h=h, s_phq=s_phq):
                        em.wait(tensor, "s_s", s_phq)
                        em.wait(tensor, "s_v")
                        em.inc(tensor.matmul(PS[4][0:64, 0:512], Y_bf[:, h, :], Phi_sb[:], start=True, stop=True), "s_t")
                        em.inc(tensor.matmul(PS[5][0:1, 0:512], w_bf[:, h:h + 1], Phi_sb[:], start=True, stop=True), "s_t")
                    t_ou = em.mark("s_t")
                    @block.vector
                    def _(vector, t_ou=t_ou):
                        em.wait(vector, "s_t", t_ou)
                        em.wait(vector, "s_v", em.inc(vector.tensor_scalar(ROWS[0:1, 1040:1552], PS[5][0:1, 0:512],
                                                                           1e-10, None, ALU.max), "s_v"))
                        em.inc(vector.reciprocal(ROWS[0:1, 1552:2064] if False else ROWS[0:1, 0:512], ROWS[0:1, 1040:1552]), "s_v")
                    v_nr = em.mark("s_v")
                    @block.tensor
                    def _(tensor, v_nr=v_nr):
                        em.wait(tensor, "s_v", v_nr)
                        em.inc(tensor.matmul(PS[6][0:64, 0:512], ones_f[0:1, 0:64], ROWS[0:1, 0:512],
                                             start=True, stop=True), "s_t")
                    t_nrb = em.mark("s_t")
                    @block.vector
                    def _(vector, oc=oc, p0=64 * par, t_nrb=t_nrb):
                        em.wait(vector, "s_t", t_nrb)
                        em.wait(vector, "s_v", em.inc(vector.tensor_copy(rnb_sb[:, 0:512], PS[6][0:64, 0:512]), "s_v"))
                        em.inc(vector.tensor_tensor(AO_fm[p0:p0 + 64, oc, :], PS[4][0:64, 0:512], rnb_sb[:, 0:512],
                                                    ALU.mult), "s_v")
                ao_done = em.mark("s_v")
                wo_i = wk_i + 3
                for oc in range(FC):
                    n2 = ct * FC + oc
                    last_mat = (n2 == 4 * FC - 1)
                    bank = PS[n2 % 2]
                    @block.tensor
                    def _(tensor, ct=ct, oc=oc, bank=bank, wo_i=wo_i, ao_done=ao_done, n2=n2, last_mat=last_mat):
                        em.wait(tensor, "s_v", ao_done)
                        if n2 % 8 >= 2 or ct > 0:
                            em.wait(tensor, "s_v")
                        if n2 == 0:
                            em.wait(tensor, "s_wa", wmarks[wo_i])
                        for ic in range(FC):
                            mm = tensor.matmul(bank[:, 0:512], wslot_ab(wo_i)[:, ic, oc * 128:(oc + 1) * 128],
                                               AO_fm[:, ic, :], start=(ic == 0), stop=(ic == FC - 1))
                            if ic == FC - 1:
                                em.inc(mm, "s_wb" if last_mat else "s_t")
                    t_wo = em.mark("s_t")
                    wb_wo = em.mark("s_wb")
                    @block.vector
                    def _(vector, ct=ct, oc=oc, bank=bank, t_wo=t_wo, wb_wo=wb_wo, last_mat=last_mat):
                        if last_mat:
                            em.wait(vector, "s_wb", wb_wo)
                        else:
                            em.wait(vector, "s_t", t_wo)
                        em.inc(vector.scalar_tensor_tensor(acc3[:, oc, ct * 512:(ct + 1) * 512], bank[:, 0:512],
                                                           bia[:, 24 + oc:25 + oc], acc3[:, oc, ct * 512:(ct + 1) * 512],
                                                           ALU.add, ALU.add), "s_v")
            wcons[0] = wk_i + 4
            p3_done = em.mark("s_v")

        if UPTO >= 5:
            emit_ffn_supertile(writeback_base=None)


        if UPTO >= 4:
            @block.sync
            def _(sync):
                em.wait(sync, "s_v")
                em.wait(sync, "s_s")
                em.wait(sync, "s_t")
                em.dinc(sync.dma_start(out=y.rearrange("(c p) t -> p c t", p=128), in_=acc3), "s_y")
                em.wait(sync, "s_y")
        if UPTO in (2, 3):
            srcf = x2f if UPTO == 2 else x3f
            @block.gpsimd
            def _(gpsimd):
                em.wait(gpsimd, "s_xa")
                em.wait(gpsimd, "s_xb")
                em.wait(gpsimd, "s_m")
                pid = gpsimd.partition_id()
                off = gpsimd.compute_val((pid % 2) * 2048)
                em.dinc(gpsimd.dma_start(out=y.rearrange("(c p) t -> p c t", p=128),
                                         in_=srcf[:, :, bass.DynSlice(off, 2048)]), "s_y")
                em.wait(gpsimd, "s_y")

    return nc


def stage_inputs(inp):
    g = {k: np.asarray(v) for k, v in inp.items()}
    cores = []
    for c in range(8):
        seq, half = c // 2, c % 2
        m = {
            "xT": np.ascontiguousarray(g["L0_attn_a_x"][seq].T),
            "halfoff": np.array([[half * 2048]], np.int32),
            "basisT": np.ascontiguousarray(g["L0_attn_a_basis"].T),
            "qcT": np.ascontiguousarray(g["L0_attn_a_q_coeffs"].T),
            "kcT": np.ascontiguousarray(g["L0_attn_a_k_coeffs"].T),
            "vcT": np.ascontiguousarray(g["L0_attn_a_v_coeffs"].T),
            "ocT": np.ascontiguousarray(g["L0_attn_a_o_coeffs"].T),
            "scal": np.array([[g["L0_attn_a_decay"], g["L0_attn_a_out_scale"],
                               g["L0_attn_b_temperature"], 0.0]], np.float32),
            "idx": g["L0_attn_b_landmark_idx"].reshape(1, 64).astype(np.int32),
            "WqT": np.ascontiguousarray(g["L0_attn_b_Wq"].T),
            "WkT": np.ascontiguousarray(g["L0_attn_b_Wk"].T),
            "WvT": np.ascontiguousarray(g["L0_attn_b_Wv"].T),
            "WoT": np.ascontiguousarray(g["L0_attn_b_Wo"].T),
            "bqp": np.ascontiguousarray(g["L0_attn_b_bq"].reshape(8, 128).T),
            "bkp": np.ascontiguousarray(g["L0_attn_b_bk"].reshape(8, 128).T),
            "bvp": np.ascontiguousarray(g["L0_attn_b_bv"].reshape(8, 128).T),
            "bop": np.ascontiguousarray(g["L0_attn_b_bo"].reshape(8, 128).T),
            "gate_wT": np.ascontiguousarray(g["L0_ffn_gate_w"].T),
            "sgwT": np.ascontiguousarray(g["L0_ffn_shared_gate_w"].T),
            "w_gate": g["L0_ffn_w_gate"],
            "w_up": g["L0_ffn_w_up"],
            "w_down": g["L0_ffn_w_down"],
            "sw_gate": g["L0_ffn_sw_gate"],
            "sw_up": g["L0_ffn_sw_up"],
            "sw_down": g["L0_ffn_sw_down"],
        }
        cores.append({k: np.ascontiguousarray(v) for k, v in m.items()})
    return cores


def kernel(**inputs):
    in_maps = stage_inputs(inputs)
    nc = build_nc()
    res = run_bass_kernel_spmd(nc, in_maps, core_ids=list(range(8)))
    out = np.empty((4, 4096, 1024), np.float32)
    for c in range(8):
        seq, half = c // 2, c % 2
        yc = res.results[c]["y"]
        out[seq, half * 2048:(half + 1) * 2048, :] = yc.T
    return out




# version 06 (final)
# speedup vs baseline: 1.0326x; 1.0326x over previous
import numpy as np
import contextlib
import concourse.bass as bass
import concourse.mybir as mybir
from concourse.bass_utils import run_bass_kernel_spmd

F32 = mybir.dt.float32
BF16 = mybir.dt.bfloat16
I32 = mybir.dt.int32
AF = mybir.ActivationFunctionType
ALU = mybir.AluOpType
AX = mybir.AxisListType

H = 1024
FC = 8
T = 4096
NCH = 32
TT = 512
NT = 8
NH, HD, NL = 16, 64, 64
I_FF = 1408
IC = 11
NEXP = 8

UPTO = 5
REFINE_FLAG = 1


class Emit:
    def __init__(self, nc):
        self.nc = nc
        self.cnt = {}
        self.sems = {}

    def sem(self, es, name):
        s = es.enter_context(self.nc.semaphore(name))
        self.sems[name] = s
        self.cnt[name] = 0
        return s

    def inc(self, inst, name, n=1):
        inst.then_inc(self.sems[name], n)
        self.cnt[name] += n
        return self.cnt[name]

    def dinc(self, inst, name):
        return self.inc(inst, name, 16)

    def mark(self, name):
        return self.cnt[name]

    def wait(self, eng, name, val=None):
        eng.wait_ge(self.sems[name], self.cnt[name] if val is None else val)


def build_nc():
    nc = bass.Bass("TRN2", target_bir_lowering=False)
    dt = nc.dram_tensor

    xT = dt("xT", [H, T], F32, kind="ExternalInput").ap()
    halfoff = dt("halfoff", [1, 1], I32, kind="ExternalInput").ap()
    basisT = dt("basisT", [64, H], F32, kind="ExternalInput").ap()
    qcT = dt("qcT", [64, 128], F32, kind="ExternalInput").ap()
    kcT = dt("kcT", [64, 128], F32, kind="ExternalInput").ap()
    vcT = dt("vcT", [64, 128], F32, kind="ExternalInput").ap()
    ocT = dt("ocT", [64, 128], F32, kind="ExternalInput").ap()
    scal = dt("scal", [1, 4], F32, kind="ExternalInput").ap()
    idx = dt("idx", [1, 64], I32, kind="ExternalInput").ap()
    WqT = dt("WqT", [H, H], F32, kind="ExternalInput").ap()
    WkT = dt("WkT", [H, H], F32, kind="ExternalInput").ap()
    WvT = dt("WvT", [H, H], F32, kind="ExternalInput").ap()
    WoT = dt("WoT", [H, H], F32, kind="ExternalInput").ap()
    bqp = dt("bqp", [128, FC], F32, kind="ExternalInput").ap()
    bkp = dt("bkp", [128, FC], F32, kind="ExternalInput").ap()
    bvp = dt("bvp", [128, FC], F32, kind="ExternalInput").ap()
    bop = dt("bop", [128, FC], F32, kind="ExternalInput").ap()
    gate_wT = dt("gate_wT", [H, NEXP], F32, kind="ExternalInput").ap()
    sgwT = dt("sgwT", [H, 1], F32, kind="ExternalInput").ap()
    w_gate = dt("w_gate", [NEXP, H, I_FF], F32, kind="ExternalInput").ap()
    w_up = dt("w_up", [NEXP, H, I_FF], F32, kind="ExternalInput").ap()
    w_down = dt("w_down", [NEXP, I_FF, H], F32, kind="ExternalInput").ap()
    sw_gate = dt("sw_gate", [H, I_FF], F32, kind="ExternalInput").ap()
    sw_up = dt("sw_up", [H, I_FF], F32, kind="ExternalInput").ap()
    sw_down = dt("sw_down", [I_FF, H], F32, kind="ExternalInput").ap()

    y = dt("y", [H, 2048], F32, kind="ExternalOutput").ap()

    x2f = dt("x2f", [128, FC, T], F32, kind="Internal").ap()
    x3f = dt("x3f", [128, FC, T], F32, kind="Internal").ap()
    x3nf = dt("x3nf", [128, FC, T], BF16, kind="Internal").ap()
    Kf = dt("Kf", [128, FC, T], BF16, kind="Internal").ap()
    Vf = dt("Vf", [128, FC, T], BF16, kind="Internal").ap()
    Qf = dt("Qf", [128, FC, T], BF16, kind="Internal").ap()


    es = contextlib.ExitStack()
    em = Emit(nc)
    with es:
        def sb(name, shp, d=F32):
            return es.enter_context(nc.sbuf_tensor(name, shp, d))

        idf = sb("idf", [128, 128], F32)
        id_bf = sb("id_bf", [128, 128], BF16)
        ones_c = sb("ones_c", [128, 2], F32)
        ones_f = sb("ones_f", [128, 128], F32)
        ones_cb = sb("ones_cb", [128, 2], BF16)
        ones64 = sb("ones64", [64, 64], BF16)
        sc_in = sb("sc_in", [1, 4], F32)
        scb = sb("scb", [128, 8], F32)
        dpow_bro = sb("dpow_bro", [128, 128], BF16)
        dpow_rev = sb("dpow_rev", [128, 1], F32)
        dt_bf = sb("dt_bf", [128, 128], BF16)
        idl_bf = sb("idl_bf", [128, 128], BF16)
        stat1 = sb("stat1", [128, 4], F32)
        iotat = sb("iotat", [128, 128], I32)
        hoff_sb = sb("hoff_sb", [1, 1], I32)

        ARX = sb("ARX", [128, FC * 2048], F32)
        ARN2 = sb("ARN2", [128, FC * 2048], BF16)
        ARH = sb("ARH", [128, 2 * 11264], BF16)
        HAR = sb("HAR", [128, IC * 2048], BF16)
        ROWS = sb("ROWS", [128, 2048], F32)
        small = sb("small", [128, 1024], BF16)
        comb = sb("comb", [16, 2048], BF16)
        oneh = sb("oneh", [16, 9 * 128], BF16)
        gw9 = sb("gw9", [128, FC, 9], BF16)
        tb = sb("tb", [128, 64], F32)
        tmp2 = sb("tmp2", [128, 512], F32)
        combb = sb("combb", [128, 512], BF16)
        bia = sb("bia", [128, 32], F32)

        ARXbf = ARX.ap().bitcast(BF16)
        qw_vc = ARXbf[:, 16384:17408].rearrange("p (c k) -> p c k", k=128)
        kw_vc = ARXbf[:, 17408:18432].rearrange("p (c k) -> p c k", k=128)
        vw_vc = ARXbf[:, 18432:19456].rearrange("p (c k) -> p c k", k=128)
        owT = ARXbf[:, 19456:20480]
        basT = ARXbf[0:64, 20480:21504]
        coefT = ARXbf[0:64, 21504:22016].rearrange("p (a k) -> p a k", k=128)
        wrow = ARX[:, 11264:12288]
        wrow2 = ARXbf[:, 24576:25600]

        PS = [es.enter_context(nc.psum_tensor(f"PS{i}", [128, 512], F32)) for i in range(8)]

        for s in ["s_m", "s_xa", "s_xb", "s_wa", "s_wb", "s_wd", "s_t", "s_v", "s_s", "s_g", "s_y"]:
            em.sem(es, s)

        block = es.enter_context(nc.Block())
        C = em.cnt

        @block.sync
        def _(sync):
            em.dinc(sync.dma_start(out=sc_in[:], in_=scal), "s_m")
            em.dinc(sync.dma_start(out=wrow[0:64, 0:H], in_=basisT), "s_m")
            for i, t in enumerate([qcT, kcT, vcT, ocT]):
                em.dinc(sync.dma_start(out=wrow[64:128, i * 128:(i + 1) * 128], in_=t), "s_m")
        p0_loads = em.mark("s_m")

        @block.gpsimd
        def _(gpsimd):
            em.inc(gpsimd.iota(iotat[:], [[1, 128]], base=0, channel_multiplier=-1), "s_g")
        p0_iota = em.mark("s_g")

        @block.vector
        def _(vector):
            em.inc(vector.memset(ones_c[:], 1.0), "s_v")
            em.inc(vector.memset(ones_f[:], 1.0), "s_v")
            em.inc(vector.memset(ones_cb[:], 1.0), "s_v")
            em.inc(vector.memset(ones64[:], 1.0), "s_v")
            em.wait(vector, "s_g", p0_iota)
            em.inc(vector.tensor_scalar(idf[:], iotat[:], 0.0, None, ALU.is_equal), "s_v")
            em.inc(vector.tensor_scalar(id_bf[:], iotat[:], 0.0, None, ALU.is_equal), "s_v")
            em.wait(vector, "s_m", p0_loads)
            em.inc(vector.tensor_copy(basT[:], wrow[0:64, 0:H]), "s_v")
            em.inc(vector.tensor_copy(coefT[:], wrow[64:128, 0:512].rearrange("p (a b) -> p a b", b=128)), "s_v")
        p0_v1 = em.mark("s_v")

        @block.scalar
        def _(scalar):
            em.wait(scalar, "s_m", p0_loads)
            em.wait(scalar, "s_s", em.inc(scalar.activation(ROWS[0:1, 0:1], sc_in[0:1, 0:1], AF.Sigmoid), "s_s"))
            em.inc(scalar.activation(ROWS[0:1, 1:2], ROWS[0:1, 0:1], AF.Ln), "s_s")
        p0_s1 = em.mark("s_s")

        @block.vector
        def _(vector):
            em.wait(vector, "s_s", p0_s1)
            em.wait(vector, "s_v", em.inc(vector.reciprocal(ROWS[0:1, 2:3], sc_in[0:1, 2:3]), "s_v"))
            em.inc(vector.tensor_scalar(ROWS[0:1, 2:3], ROWS[0:1, 2:3], -1.0, None, ALU.mult), "s_v")
            em.inc(vector.tensor_copy(ROWS[0:1, 3:4], sc_in[0:1, 1:2]), "s_v")
            em.inc(vector.memset(ROWS[0:1, 4:5], 1.0 / 1024.0), "s_v")
            em.inc(vector.tensor_scalar(ROWS[0:1, 5:6], ROWS[0:1, 1:2], 128.0, None, ALU.mult), "s_v")
        p0_v2 = em.mark("s_v")

        @block.scalar
        def _(scalar):
            em.wait(scalar, "s_v", p0_v2)
            em.inc(scalar.activation(ROWS[0:1, 5:6], ROWS[0:1, 5:6], AF.Exp), "s_s")
        p0_s2 = em.mark("s_s")

        @block.tensor
        def _(tensor):
            em.wait(tensor, "s_s", p0_s2)
            tensor.nop()
        @block.vector
        def _(vector):
            em.wait(vector, "s_s", p0_s2)
            em.inc(vector.tensor_copy(ROWS[0:1, 8:9], ROWS[0:1, 3:4]), "s_v")
            em.inc(vector.tensor_copy(ROWS[0:1, 9:10], ROWS[0:1, 1:2]), "s_v")
            em.inc(vector.tensor_copy(ROWS[0:1, 10:11], ROWS[0:1, 2:3]), "s_v")
            em.inc(vector.tensor_copy(ROWS[0:1, 11:12], ROWS[0:1, 5:6]), "s_v")
            em.inc(vector.tensor_copy(ROWS[0:1, 12:13], ROWS[0:1, 4:5]), "s_v")
            em.inc(vector.memset(ROWS[0:1, 13:14], 1e-5), "s_v")
        p0_v3 = em.mark("s_v")

        @block.tensor
        def _(tensor):
            em.wait(tensor, "s_v", p0_v3)
            em.inc(tensor.matmul(PS[0][:, 0:6], ones_f[0:1, 0:128], ROWS[0:1, 8:14], start=True, stop=True), "s_t")
        p0_t1 = em.mark("s_t")

        @block.vector
        def _(vector):
            em.wait(vector, "s_t", p0_t1)
            em.inc(vector.tensor_copy(scb[:, 0:6], PS[0][:, 0:6]), "s_v")
        p0_v4 = em.mark("s_v")

        @block.gpsimd
        def _(gpsimd):
            em.wait(gpsimd, "s_v", p0_v1)
            gpsimd.iota(iotat[:, 0:1], [[0, 1]], base=127, channel_multiplier=-1)
            em.inc(gpsimd.iota(iotat[:, 1:2], [[0, 1]], base=0, channel_multiplier=1), "s_g")
        p0_g2 = em.mark("s_g")

        @block.vector
        def _(vector):
            em.wait(vector, "s_g", p0_g2)
            em.wait(vector, "s_v", p0_v4)
            em.inc(vector.tensor_copy(ROWS[:, 16:17], iotat[:, 0:1]), "s_v")
            em.inc(vector.tensor_copy(ROWS[:, 17:18], iotat[:, 1:2]), "s_v")
        p0_v5 = em.mark("s_v")

        @block.scalar
        def _(scalar):
            em.wait(scalar, "s_v", p0_v5)
            scalar.activation(dpow_rev[:], ROWS[:, 16:17], AF.Exp, scale=scb[:, 1:2])
            em.inc(scalar.activation(ROWS[:, 18:19], ROWS[:, 17:18], AF.Exp, scale=scb[:, 1:2]), "s_s")
        p0_s3 = em.mark("s_s")

        @block.tensor
        def _(tensor):
            em.wait(tensor, "s_s", p0_s3)
            em.inc(tensor.transpose(PS[1][0:1, 0:128], ROWS[:, 18:19], idf[:]), "s_t")
        p0_t2 = em.mark("s_t")

        @block.vector
        def _(vector):
            em.wait(vector, "s_t", p0_t2)
            em.inc(vector.tensor_copy(ROWS[0:1, 128:256], PS[1][0:1, 0:128]), "s_v")
        p0_v6 = em.mark("s_v")

        @block.tensor
        def _(tensor):
            em.wait(tensor, "s_v", p0_v6)
            em.inc(tensor.matmul(PS[2][:, 0:128], ones_f[0:1, 0:128], ROWS[0:1, 128:256], start=True, stop=True), "s_t")
        p0_t3 = em.mark("s_t")

        @block.vector
        def _(vector):
            em.wait(vector, "s_t", p0_t3)
            em.inc(vector.tensor_copy(dpow_bro[:], PS[2][:, 0:128]), "s_v")
        p0_v7 = em.mark("s_v")

        @block.gpsimd
        def _(gpsimd):
            em.wait(gpsimd, "s_v", p0_v5)
            em.inc(gpsimd.iota(iotat[:], [[1, 128]], base=-1, channel_multiplier=-1), "s_g")
        p0_g3 = em.mark("s_g")

        @block.vector
        def _(vector):
            em.wait(vector, "s_g", p0_g3)
            em.wait(vector, "s_v", em.inc(vector.tensor_copy(wrow[:, 0:128], iotat[:]), "s_v"))
            em.inc(vector.tensor_scalar(wrow[:, 128:256], wrow[:, 0:128], 0.0, None, ALU.max), "s_v")
            em.inc(vector.tensor_scalar(wrow[:, 256:384], wrow[:, 0:128], 0.0, None, ALU.is_ge), "s_v")
        p0_v8 = em.mark("s_v")

        @block.scalar
        def _(scalar):
            em.wait(scalar, "s_v", p0_v8)
            em.inc(scalar.activation(wrow[:, 384:512], wrow[:, 128:256], AF.Exp, scale=scb[:, 1:2]), "s_s")
        p0_s4 = em.mark("s_s")

        @block.vector
        def _(vector):
            em.wait(vector, "s_s", p0_s4)
            em.inc(vector.tensor_tensor(dt_bf[:], wrow[:, 384:512], wrow[:, 256:384], ALU.mult), "s_v")
            em.inc(vector.tensor_scalar(idl_bf[:], idf[:], scb[:, 3:4], None, ALU.mult), "s_v")
        p0_v9 = em.mark("s_v")

        for wi, (dst, use_soft) in enumerate([(qw_vc, True), (kw_vc, True), (vw_vc, True), (owT, False)]):
            @block.tensor
            def _(tensor, wi=wi):
                em.wait(tensor, "s_v", p0_v1)
                em.wait(tensor, "s_s")
                em.wait(tensor, "s_v")
                for half, bank in enumerate([PS[3], PS[5]]):
                    em.inc(tensor.matmul(bank[:, 0:512], coefT[:, wi, :],
                                         basT[:, half * 512:(half + 1) * 512], start=True, stop=True), "s_t")
            t_mm = em.mark("s_t")

            if use_soft:
                @block.vector
                def _(vector):
                    em.wait(vector, "s_t", t_mm)
                    em.inc(vector.reduce_max(stat1[:, 0:1], PS[3].ap(), axis=AX.X), "s_v")
                    em.wait(vector, "s_v", em.inc(vector.reduce_max(stat1[:, 1:2], PS[5].ap(), axis=AX.X), "s_v"))
                    em.wait(vector, "s_v", em.inc(vector.tensor_tensor(stat1[:, 0:1], stat1[:, 0:1], stat1[:, 1:2], ALU.max), "s_v"))
                    em.inc(vector.tensor_scalar(stat1[:, 0:1], stat1[:, 0:1], -1.0, None, ALU.mult), "s_v")
                v_mx = em.mark("s_v")

                @block.scalar
                def _(scalar):
                    em.wait(scalar, "s_v", v_mx)
                    em.inc(scalar.activation(wrow[:, 0:512], PS[3].ap(), AF.Exp, bias=stat1[:, 0:1],
                                             accum_out=stat1[:, 1:2]), "s_s")
                    em.inc(scalar.activation(wrow[:, 512:1024], PS[5].ap(), AF.Exp, bias=stat1[:, 0:1],
                                             accum_out=stat1[:, 2:3]), "s_s")
                s_exp = em.mark("s_s")

                @block.vector
                def _(vector):
                    em.wait(vector, "s_s", s_exp)
                    em.wait(vector, "s_v", em.inc(vector.tensor_tensor(stat1[:, 1:2], stat1[:, 1:2], stat1[:, 2:3], ALU.add), "s_v"))
                    em.wait(vector, "s_v", em.inc(vector.reciprocal(stat1[:, 0:1], stat1[:, 1:2]), "s_v"))
                    em.inc(vector.tensor_scalar(wrow2[:, 0:H], wrow[:, 0:H], stat1[:, 0:1], None, ALU.mult), "s_v")
                v_sm = em.mark("s_v")
            else:
                @block.vector
                def _(vector):
                    em.wait(vector, "s_t", t_mm)
                    em.inc(vector.tensor_copy(wrow2[:, 0:512], PS[3].ap()), "s_v")
                    em.inc(vector.tensor_copy(wrow2[:, 512:1024], PS[5].ap()), "s_v")
                v_sm = em.mark("s_v")

            if dst is owT:
                @block.vector
                def _(vector):
                    em.wait(vector, "s_v", v_sm)
                    em.inc(vector.tensor_copy(owT[:], wrow2[:, 0:H]), "s_v")
            else:
                @block.tensor
                def _(tensor):
                    em.wait(tensor, "s_v", v_sm)
                    for fcx in range(FC):
                        em.inc(tensor.transpose(PS[4].ap().bitcast(BF16)[:, fcx * 128:(fcx + 1) * 128],
                                                wrow2[:, fcx * 128:(fcx + 1) * 128], id_bf[:]), "s_t")
                t_tr = em.mark("s_t")

                @block.vector
                def _(vector, dst=dst):
                    em.wait(vector, "s_t", t_tr)
                    em.inc(vector.tensor_copy(dst[:], PS[4].ap().bitcast(BF16)[:, 0:1024].rearrange(
                        "p (a b) -> p a b", b=128)), "s_v")

        p0_done_v = em.mark("s_v")
        p0_done_t = em.mark("s_t")

        for e in range(9):
            @block.gpsimd
            def _(gpsimd, e=e):
                if e == 0:
                    em.wait(gpsimd, "s_v", p0_v8)
                else:
                    em.wait(gpsimd, "s_v")
                em.inc(gpsimd.iota(iotat[0:16, 0:128], [[0, 128]], base=-e, channel_multiplier=1), "s_g")

            @block.vector
            def _(vector, e=e):
                em.wait(vector, "s_g")
                em.inc(vector.tensor_scalar(oneh[0:16, e * 128:(e + 1) * 128], iotat[0:16, 0:128],
                                            0.0, None, ALU.is_equal), "s_v")
        @block.vector
        def _(vector):
            em.inc(vector.memset(comb[0:16, :], 0.0), "s_v")
        @block.gpsimd
        def _(gpsimd):
            em.wait(gpsimd, "s_wa")
            em.dinc(gpsimd.dma_start(out=gw9[:, :, 0:8], in_=gate_wT.rearrange("(c p) e -> p c e", p=128),
                                     allow_slow_non_contiguous=True), "s_wa")
            em.dinc(gpsimd.dma_start(out=gw9[:, :, 8:9], in_=sgwT.rearrange("(c p) o -> p c o", p=128),
                                     allow_slow_non_contiguous=True), "s_wa")
        @block.sync
        def _(sync):
            em.wait(sync, "s_m")
            for i, bap in enumerate([bqp, bkp, bvp, bop]):
                em.dinc(sync.dma_start(out=bia[:, i * 8:(i + 1) * 8], in_=bap), "s_m")
            em.dinc(sync.dma_start(out=iotat[0:1, 0:64].bitcast(I32) if False else hoff_sb[0:1, 0:1], in_=halfoff), "s_m")
        p0b_gw = em.mark("s_wa")
        p0b_v = em.mark("s_v")

        def wslot_gu(k):
            return ARH[:, (k % 2) * 11264:(k % 2) * 11264 + 11264].rearrange("p (c i) -> p c i", c=FC)
        def wslot_dn(k):
            return ARH[:, (k % 2) * 11264:(k % 2) * 11264 + 11264].rearrange("p (i c) -> p i c", i=IC)
        def expert_mats(e):
            if e < 8:
                return (w_gate[e:e + 1].squeeze(0).rearrange("(c p) i -> p c i", p=128),
                        w_up[e:e + 1].squeeze(0).rearrange("(c p) i -> p c i", p=128),
                        w_down[e:e + 1].squeeze(0).rearrange("(i p) c -> p i c", p=128))
            return (sw_gate.rearrange("(c p) i -> p c i", p=128),
                    sw_up.rearrange("(c p) i -> p c i", p=128),
                    sw_down.rearrange("(i p) c -> p i c", p=128))
        def wslot_ab(k):
            return ARH[:, (k % 2) * 11264:(k % 2) * 11264 + 8192].rearrange("p (c o) -> p c o", c=FC)
        wstream = []
        for g in range(2):
            for e in range(9):
                gm, um, dm = expert_mats(e)
                wstream += [(gm, "gu"), (um, "gu"), (dm, "dn")]
        n_ffn1_mats = len(wstream)
        if UPTO >= 4:
            for Wt in [WkT, WvT, WqT, WoT]:
                wstream.append((Wt.rearrange("(c p) o -> p c o", p=128), "ab"))
        if UPTO >= 5:
            for e in range(9):
                gm, um, dm = expert_mats(e)
                wstream += [(gm, "gu"), (um, "gu"), (dm, "dn")]

        wmarks = {}
        def emit_wloads(lo, hi):
            @block.gpsimd
            def _(gpsimd):
                for idx in range(lo, hi):
                    em.wait(gpsimd, "s_wa")
                    if idx >= 2:
                        em.wait(gpsimd, "s_wb", idx - 1)
                    ap, kind = wstream[idx]
                    dst = {"gu": wslot_gu, "dn": wslot_dn, "ab": wslot_ab}[kind](idx)
                    em.dinc(gpsimd.dma_start(out=dst, in_=ap), "s_wa")
                    wmarks[idx] = em.mark("s_wa")
        lidx_ld = p3_g = 0
        if UPTO >= 4:
            @block.sync
            def _(sync):
                em.wait(sync, "s_m")
                em.wait(sync, "s_v", p0b_v)
                em.dinc(sync.dma_start(out=iotat[0:1, 0:64], in_=idx), "s_m")
            lidx_ld = em.mark("s_m")
            @block.gpsimd
            def _(gpsimd):
                em.wait(gpsimd, "s_v", p0b_v)
                em.inc(gpsimd.iota(iotat[:, 64:65], [[0, 1]], base=0, channel_multiplier=1), "s_g")
            p3_g = em.mark("s_g")
        if UPTO >= 3:
            emit_wloads(0, min(len(wstream), n_ffn1_mats + 5))

        xt_v = ARX.ap().rearrange("p (b c t) -> p b c t", b=2, c=FC)[:, :, :, 0:512]
        xt = ARX.ap().rearrange("p (b x) -> p b x", b=4)
        def xt_buf(j):
            return ARX.ap().rearrange("p (b c t) -> p b c t", b=4, c=FC)[:, j % 2, :, :]
        def x1n_buf(j):
            return ARN2.ap().rearrange("p (b c t) -> p b c t", b=4, c=FC)[:, j % 2, :, :]
        def xbf_buf(j):
            return ARN2.ap().rearrange("p (b c t) -> p b c t", b=4, c=FC)[:, 2, :, :]
        def sqb_buf(j):
            return ARN2.ap().rearrange("p (b c t) -> p b c t", b=4, c=FC)[:, 3, :, :]
        sm = small.ap()
        M_bf = sm[:, 0:128]
        q_bf = sm[:, 128:256]
        qd_bf = sm[:, 256:384]
        k_bf = sm[:, 384:512]
        vt_bf = sm[:, 512:640]
        sdt_bf = sm[:, 640:768]
        kd_bf = sm[:, 768:896]
        retr_bf = sm[:, 896:1024]

        @block.vector
        def _(vector):
            em.inc(vector.memset(M_bf, 0.0), "s_v")
        p1_m0 = em.mark("s_v")
        qkv_marks = []

        if UPTO >= 1:
            for j in range(NT):
                psem = "s_xa" if j % 2 == 0 else "s_xb"
                @block.sync
                def _(sync, j=j, psem=psem):
                    if j >= 2:
                        em.wait(sync, "s_xa" if j % 2 == 0 else "s_xb")
                    em.dinc(sync.dma_start(out=xt_buf(j), in_=xT.rearrange("(c p) t -> p c t", p=128)[:, :, j * 512:(j + 1) * 512]), psem)
                ld_mark = em.mark(psem)

                @block.vector
                def _(vector, j=j, psem=psem, ld_mark=ld_mark):
                    em.wait(vector, psem, ld_mark)
                    em.wait(vector, "s_t")
                    em.inc(vector.tensor_copy(xbf_buf(j), xt_buf(j)), "s_v")
                v_cast = em.mark("s_v")

                @block.scalar
                def _(scalar, j=j, psem=psem, ld_mark=ld_mark):
                    em.wait(scalar, psem, ld_mark)
                    em.wait(scalar, "s_t")
                    em.inc(scalar.activation(sqb_buf(j), xt_buf(j), AF.Square), "s_s")
                s_sq = em.mark("s_s")

                @block.tensor
                def _(tensor, j=j, v_cast=v_cast, s_sq=s_sq):
                    em.wait(tensor, "s_v", v_cast)
                    em.wait(tensor, "s_s", s_sq)
                    for fcx in range(FC):
                        mm = tensor.matmul(PS[0][0:1, 0:512], ones_cb[:, 0:1], xbf_buf(j)[:, fcx, :],
                                           start=(fcx == 0), stop=(fcx == FC - 1))
                        if fcx == FC - 1:
                            em.inc(mm, "s_t")
                    for fcx in range(FC):
                        mm = tensor.matmul(PS[1][0:1, 0:512], ones_cb[:, 0:1], sqb_buf(j)[:, fcx, :],
                                           start=(fcx == 0), stop=(fcx == FC - 1))
                        if fcx == FC - 1:
                            em.inc(mm, "s_t")
                t_stat = em.mark("s_t")

                @block.vector
                def _(vector, t_stat=t_stat):
                    em.wait(vector, "s_t", t_stat)
                    r = ROWS
                    em.inc(vector.tensor_scalar(r[0:1, 256:768], PS[0][0:1, 0:512], -1.0 / 1024.0, None, ALU.mult), "s_v")
                    em.wait(vector, "s_v", em.inc(vector.tensor_scalar(r[0:1, 768:1280], PS[1][0:1, 0:512], 1.0 / 1024.0, None, ALU.mult), "s_v"))
                    em.wait(vector, "s_v", em.inc(vector.tensor_tensor(r[0:1, 1280:1792], r[0:1, 256:768], r[0:1, 256:768], ALU.mult), "s_v"))
                    em.inc(vector.tensor_tensor(r[0:1, 768:1280], r[0:1, 768:1280], r[0:1, 1280:1792], ALU.subtract), "s_v")
                v_var = em.mark("s_v")

                @block.scalar
                def _(scalar, v_var=v_var):
                    em.wait(scalar, "s_v", v_var)
                    em.inc(scalar.activation(ROWS[0:1, 1280:1792], ROWS[0:1, 768:1280], AF.Sqrt, bias=scb[0:1, 5:6]), "s_s")
                s_std = em.mark("s_s")

                @block.vector
                def _(vector, s_std=s_std):
                    em.wait(vector, "s_s", s_std)
                    em.wait(vector, "s_v", em.inc(vector.reciprocal(ROWS[0:1, 768:1280], ROWS[0:1, 1280:1792]), "s_v"))
                    em.inc(vector.tensor_tensor(ROWS[0:1, 1280:1792], ROWS[0:1, 256:768], ROWS[0:1, 768:1280], ALU.mult), "s_v")
                v_rstd = em.mark("s_v")

                @block.tensor
                def _(tensor, v_rstd=v_rstd):
                    em.wait(tensor, "s_v", v_rstd)
                    em.inc(tensor.matmul(PS[0][:, 0:512], ones_f[0:1, 0:128], ROWS[0:1, 768:1280], start=True, stop=True), "s_t")
                    em.inc(tensor.matmul(PS[1][:, 0:512], ones_f[0:1, 0:128], ROWS[0:1, 1280:1792], start=True, stop=True), "s_t")
                t_bro = em.mark("s_t")

                @block.vector
                def _(vector, j=j, t_bro=t_bro):
                    em.wait(vector, "s_t", t_bro)
                    for half in range(2):
                        em.wait(vector, "s_v")
                        for fcx in range(half * 4, half * 4 + 4):
                            em.inc(vector.tensor_tensor(ROWS[:, (fcx % 4) * 512:(fcx % 4 + 1) * 512],
                                                        xt_buf(j)[:, fcx, :], PS[0][:, 0:512], ALU.mult), "s_v")
                        em.wait(vector, "s_v")
                        for fcx in range(half * 4, half * 4 + 4):
                            em.inc(vector.tensor_tensor(x1n_buf(j)[:, fcx, :], ROWS[:, (fcx % 4) * 512:(fcx % 4 + 1) * 512],
                                                        PS[1][:, 0:512], ALU.add), "s_v")
                v_ln = em.mark("s_v")

                for cc in range(4):
                    ch = j * 4 + cc
                    tsl = slice(cc * 128, (cc + 1) * 128)
                    @block.tensor
                    def _(tensor, j=j, cc=cc, tsl=tsl, v_ln=v_ln):
                        em.wait(tensor, "s_v", max([v_ln] + qkv_marks[-1:]))
                        for fcx in range(FC):
                            mm = tensor.matmul(PS[0][:, 0:128], qw_vc[:, fcx, :], x1n_buf(j)[:, fcx, tsl],
                                               start=(fcx == 0), stop=(fcx == FC - 1))
                            if fcx == FC - 1:
                                em.inc(mm, "s_t")
                        for fcx in range(FC):
                            mm = tensor.matmul(PS[1][:, 0:128], kw_vc[:, fcx, :], x1n_buf(j)[:, fcx, tsl],
                                               start=(fcx == 0), stop=(fcx == FC - 1))
                            if fcx == FC - 1:
                                em.inc(mm, "s_t")
                        for fcx in range(FC):
                            mm = tensor.matmul(PS[2][:, 0:128], x1n_buf(j)[:, fcx, tsl], vw_vc[:, fcx, :],
                                               start=(fcx == 0), stop=(fcx == FC - 1))
                            if fcx == FC - 1:
                                em.inc(mm, "s_t")
                    t_qkv = em.mark("s_t")

                    @block.vector
                    def _(vector, t_qkv=t_qkv):
                        em.wait(vector, "s_t", t_qkv)
                        em.inc(vector.tensor_copy(q_bf, PS[0][:, 0:128]), "s_v")
                        em.inc(vector.tensor_tensor(qd_bf, PS[0][:, 0:128], dpow_bro[:], ALU.mult), "s_v")
                        em.inc(vector.tensor_copy(k_bf, PS[1][:, 0:128]), "s_v")
                        em.inc(vector.tensor_copy(vt_bf, PS[2][:, 0:128]), "s_v")
                    v_qkv = em.mark("s_v")
                    qkv_marks.append(v_qkv)

                    @block.tensor
                    def _(tensor, v_qkv=v_qkv):
                        em.wait(tensor, "s_v", v_qkv)
                        em.inc(tensor.matmul(PS[3][:, 0:128], k_bf, q_bf, start=True, stop=True), "s_t")
                        em.inc(tensor.transpose(PS[4].ap().bitcast(BF16)[:, 0:128], k_bf, id_bf[:]), "s_t")
                    t_st = em.mark("s_t")

                    @block.vector
                    def _(vector, t_st=t_st):
                        em.wait(vector, "s_t", t_st)
                        em.inc(vector.tensor_tensor(sdt_bf, PS[3][:, 0:128], dt_bf[:], ALU.mult), "s_v")
                        em.inc(vector.tensor_scalar(kd_bf, PS[4].ap().bitcast(BF16)[:, 0:128], dpow_rev[:], None, ALU.mult), "s_v")
                    v_sdt = em.mark("s_v")

                    @block.tensor
                    def _(tensor, v_sdt=v_sdt):
                        em.wait(tensor, "s_v", v_sdt)
                        tensor.matmul(PS[5][:, 0:128], vt_bf, sdt_bf, start=True, stop=False)
                        em.inc(tensor.matmul(PS[5][:, 0:128], M_bf, qd_bf, start=False, stop=True), "s_t")
                        tensor.matmul(PS[6][:, 0:128], idl_bf[:], M_bf, start=True, stop=False)
                        em.inc(tensor.matmul(PS[6][:, 0:128], kd_bf, vt_bf, start=False, stop=True), "s_t")
                    t_ret = em.mark("s_t")

                    @block.vector
                    def _(vector, t_ret=t_ret):
                        em.wait(vector, "s_t", t_ret)
                        em.inc(vector.tensor_copy(retr_bf, PS[5][:, 0:128]), "s_v")
                        em.inc(vector.tensor_copy(M_bf, PS[6][:, 0:128]), "s_v")
                    v_ret = em.mark("s_v")

                    @block.tensor
                    def _(tensor, v_ret=v_ret):
                        em.wait(tensor, "s_v", v_ret)
                        for vc in range(FC):
                            b = PS[7] if vc < 4 else PS[3]
                            em.inc(tensor.matmul(b[:, (vc % 4) * 128:(vc % 4 + 1) * 128], owT[:, vc * 128:(vc + 1) * 128],
                                                 retr_bf, start=True, stop=True), "s_t")
                    t_out = em.mark("s_t")

                    @block.vector
                    def _(vector, j=j, tsl=tsl, t_out=t_out):
                        em.wait(vector, "s_t", t_out)
                        for vc in range(FC):
                            b = PS[7] if vc < 4 else PS[3]
                            em.inc(vector.scalar_tensor_tensor(xt_buf(j)[:, vc, tsl], b[:, (vc % 4) * 128:(vc % 4 + 1) * 128],
                                                               scb[:, 0:1], xt_buf(j)[:, vc, tsl], ALU.mult, ALU.add), "s_v")

                v_tile_done = em.mark("s_v")
                @block.sync
                def _(sync, j=j, psem=psem, v_tile_done=v_tile_done):
                    em.wait(sync, "s_v", v_tile_done)
                    em.dinc(sync.dma_start(out=x2f[:, :, j * 512:(j + 1) * 512], in_=xt_buf(j)), psem)

        acc3 = ARX.ap().rearrange("p (c t) -> p c t", c=FC)
        xn3 = ARN2.ap().rearrange("p (c t) -> p c t", c=FC)
        h3 = HAR.ap().rearrange("p (i t) -> p i t", i=IC)
        hbf = HAR[:, 0:4096].rearrange("p (c t) -> p c t", c=FC)
        sqb2 = HAR[:, 4096:8192].rearrange("p (c t) -> p c t", c=FC)
        wcons = [0]

        def emit_ln2(src, dst):
            @block.vector
            def _(vector):
                em.wait(vector, "s_t")
                em.wait(vector, "s_m")
                em.inc(vector.tensor_copy(hbf[:, :, 0:512], src), "s_v")
            v_c = em.mark("s_v")

            @block.scalar
            def _(scalar):
                em.wait(scalar, "s_t")
                em.wait(scalar, "s_m")
                em.wait(scalar, "s_v", v_c)
                em.inc(scalar.activation(sqb2[:, :, 0:512], src, AF.Square), "s_s")
            s_q = em.mark("s_s")

            @block.tensor
            def _(tensor):
                em.wait(tensor, "s_v")
                em.wait(tensor, "s_s", s_q)
                for fcx in range(FC):
                    mm = tensor.matmul(PS[0][0:1, 0:512], ones_cb[:, 0:1], hbf[:, fcx, 0:512],
                                       start=(fcx == 0), stop=(fcx == FC - 1))
                    if fcx == FC - 1:
                        em.inc(mm, "s_t")
                for fcx in range(FC):
                    mm = tensor.matmul(PS[1][0:1, 0:512], ones_cb[:, 0:1], sqb2[:, fcx, 0:512],
                                       start=(fcx == 0), stop=(fcx == FC - 1))
                    if fcx == FC - 1:
                        em.inc(mm, "s_t")
            t_s = em.mark("s_t")

            @block.vector
            def _(vector):
                em.wait(vector, "s_t", t_s)
                r = ROWS
                em.inc(vector.tensor_scalar(r[0:1, 256:768], PS[0][0:1, 0:512], -1.0 / 1024.0, None, ALU.mult), "s_v")
                em.wait(vector, "s_v", em.inc(vector.tensor_scalar(r[0:1, 768:1280], PS[1][0:1, 0:512], 1.0 / 1024.0, None, ALU.mult), "s_v"))
                em.wait(vector, "s_v", em.inc(vector.tensor_tensor(r[0:1, 1280:1792], r[0:1, 256:768], r[0:1, 256:768], ALU.mult), "s_v"))
                em.inc(vector.tensor_tensor(r[0:1, 768:1280], r[0:1, 768:1280], r[0:1, 1280:1792], ALU.subtract), "s_v")
            v_v = em.mark("s_v")

            @block.scalar
            def _(scalar):
                em.wait(scalar, "s_v", v_v)
                em.inc(scalar.activation(ROWS[0:1, 1280:1792], ROWS[0:1, 768:1280], AF.Sqrt, bias=scb[0:1, 5:6]), "s_s")
            s_r = em.mark("s_s")

            @block.vector
            def _(vector):
                em.wait(vector, "s_s", s_r)
                em.wait(vector, "s_v", em.inc(vector.reciprocal(ROWS[0:1, 768:1280], ROWS[0:1, 1280:1792]), "s_v"))
                em.inc(vector.tensor_tensor(ROWS[0:1, 1280:1792], ROWS[0:1, 256:768], ROWS[0:1, 768:1280], ALU.mult), "s_v")
            v_r = em.mark("s_v")

            @block.tensor
            def _(tensor):
                em.wait(tensor, "s_v", v_r)
                em.inc(tensor.matmul(PS[0][:, 0:512], ones_f[0:1, 0:128], ROWS[0:1, 768:1280], start=True, stop=True), "s_t")
                em.inc(tensor.matmul(PS[1][:, 0:512], ones_f[0:1, 0:128], ROWS[0:1, 1280:1792], start=True, stop=True), "s_t")
            t_b = em.mark("s_t")

            @block.vector
            def _(vector):
                em.wait(vector, "s_t", t_b)
                for half in range(2):
                    em.wait(vector, "s_v")
                    for fcx in range(half * 4, half * 4 + 4):
                        em.inc(vector.tensor_tensor(ROWS[:, (fcx % 4) * 512:(fcx % 4 + 1) * 512],
                                                    src[:, fcx, :], PS[0][:, 0:512], ALU.mult), "s_v")
                    em.wait(vector, "s_v")
                    for fcx in range(half * 4, half * 4 + 4):
                        em.inc(vector.tensor_tensor(dst[:, fcx, :], ROWS[:, (fcx % 4) * 512:(fcx % 4 + 1) * 512],
                                                    PS[1][:, 0:512], ALU.add), "s_v")
            return em.mark("s_v")

        def emit_router(sub):
            ssl = slice(sub * 512, (sub + 1) * 512)
            @block.tensor
            def _(tensor):
                em.wait(tensor, "s_wa", p0b_gw)
                em.wait(tensor, "s_v")
                for fcx in range(FC):
                    mm = tensor.matmul(PS[6][0:9, 0:512], gw9[:, fcx, :], xn3[:, fcx, ssl],
                                       start=(fcx == 0), stop=(fcx == FC - 1))
                    if fcx == FC - 1:
                        em.inc(mm, "s_t")
            t_l = em.mark("s_t")
            @block.vector
            def _(vector):
                em.wait(vector, "s_t", t_l)
                em.inc(vector.tensor_copy(ROWS[0:9, 0:512], PS[6][0:9, 0:512]), "s_v")
            for q in range(4):
                qsl = slice(sub * 512 + q * 128, sub * 512 + (q + 1) * 128)
                @block.tensor
                def _(tensor, qsl=qsl):
                    em.wait(tensor, "s_v")
                    em.wait(tensor, "s_s")
                    em.inc(tensor.transpose(PS[7][:, 0:9], ROWS[0:9, (qsl.start - sub * 512):(qsl.start - sub * 512) + 128], idf[0:9, 0:9]), "s_t")
                t_t = em.mark("s_t")
                @block.vector
                def _(vector, t_t=t_t):
                    em.wait(vector, "s_t", t_t)
                    l8 = PS[7][:, 0:8]
                    em.inc(vector.reduce_max(tb[:, 0:1], l8, axis=AX.X), "s_v")
                    em.wait(vector, "s_v", em.inc(vector.reduce_max(tb[:, 1:2], l8, axis=AX.X, negate=True), "s_v"))
                    em.wait(vector, "s_v", em.inc(vector.tensor_scalar(tb[:, 8:16], l8, tb[:, 0:1], None, ALU.is_equal), "s_v"))
                    em.wait(vector, "s_v", em.inc(vector.scalar_tensor_tensor(tb[:, 16:24], tb[:, 8:16], -1e30, l8, ALU.mult, ALU.add), "s_v"))
                    em.wait(vector, "s_v", em.inc(vector.reduce_max(tb[:, 2:3], tb[:, 16:24], axis=AX.X), "s_v"))
                    em.inc(vector.tensor_scalar(tb[:, 24:32], l8, tb[:, 2:3], None, ALU.is_ge), "s_v")
                v_t2 = em.mark("s_v")
                @block.scalar
                def _(scalar, v_t2=v_t2):
                    em.wait(scalar, "s_v", v_t2)
                    em.inc(scalar.activation(tb[:, 32:40], PS[7][:, 0:8], AF.Exp, bias=tb[:, 1:2]), "s_s")
                    em.inc(scalar.activation(tb[:, 56:57], PS[7][:, 8:9], AF.Sigmoid), "s_s")
                s_e = em.mark("s_s")
                @block.vector
                def _(vector, s_e=s_e):
                    em.wait(vector, "s_s", s_e)
                    em.wait(vector, "s_v", em.inc(vector.tensor_tensor(tb[:, 40:48], tb[:, 32:40], tb[:, 24:32], ALU.mult), "s_v"))
                    em.wait(vector, "s_v", em.inc(vector.reduce_sum(tb[:, 3:4], tb[:, 40:48], axis=AX.X), "s_v"))
                    em.wait(vector, "s_v", em.inc(vector.reciprocal(tb[:, 4:5], tb[:, 3:4]), "s_v"))
                    em.inc(vector.tensor_scalar(tb[:, 48:56], tb[:, 40:48], tb[:, 4:5], None, ALU.mult), "s_v")
                v_cb = em.mark("s_v")
                @block.tensor
                def _(tensor, q=q, sub=sub, v_cb=v_cb):
                    em.wait(tensor, "s_v", v_cb)
                    em.wait(tensor, "s_s")
                    em.inc(tensor.transpose(PS[6][0:9, q * 128:(q + 1) * 128], tb[:, 48:57], idf[:]), "s_t")
                t_tb = em.mark("s_t")
                @block.vector
                def _(vector, q=q, qsl=qsl, t_tb=t_tb):
                    em.wait(vector, "s_t", t_tb)
                    em.inc(vector.tensor_copy(comb[0:9, qsl], PS[6][0:9, q * 128:(q + 1) * 128]), "s_v")

        def emit_expert(e):
            gi = wcons[0]
            smarks = []
            for sub in range(4):
                for ic in range(IC):
                    n = sub * IC + ic
                    last_mat = (n == 4 * IC - 1)
                    bank = PS[n % 2]
                    @block.tensor
                    def _(tensor, n=n, ic=ic, sub=sub, bank=bank, gi=gi, last_mat=last_mat):
                        if n == 0:
                            em.wait(tensor, "s_wa", wmarks[gi])
                            em.wait(tensor, "s_v")
                            em.wait(tensor, "s_s")
                        if n >= 2:
                            em.wait(tensor, "s_s", smarks[n - 2])
                        for fcx in range(FC):
                            mm = tensor.matmul(bank[:, 0:512], wslot_gu(gi)[:, fcx, ic * 128:(ic + 1) * 128],
                                               xn3[:, fcx, sub * 512:(sub + 1) * 512],
                                               start=(fcx == 0), stop=(fcx == FC - 1))
                            if fcx == FC - 1:
                                em.inc(mm, "s_wb" if last_mat else "s_t")
                    t_g = em.mark("s_t")
                    wb_g = em.mark("s_wb")
                    @block.scalar
                    def _(scalar, ic=ic, sub=sub, bank=bank, t_g=t_g, wb_g=wb_g, last_mat=last_mat):
                        if last_mat:
                            em.wait(scalar, "s_wb", wb_g)
                        else:
                            em.wait(scalar, "s_t", t_g)
                        em.inc(scalar.activation(h3[:, ic, sub * 512:(sub + 1) * 512], bank[:, 0:512], AF.Silu), "s_s")
                    smarks.append(em.mark("s_s"))
            wcons[0] += 1
            ui = wcons[0]
            vmarks = []
            for sub in range(4):
                for ic in range(IC):
                    n = sub * IC + ic
                    last_mat = (n == 4 * IC - 1)
                    bank = PS[2 + n % 2]
                    @block.tensor
                    def _(tensor, n=n, ic=ic, sub=sub, bank=bank, ui=ui, last_mat=last_mat):
                        if n == 0:
                            em.wait(tensor, "s_wa", wmarks[ui])
                        if n >= 2:
                            em.wait(tensor, "s_v", vmarks[n - 2])
                        for fcx in range(FC):
                            mm = tensor.matmul(bank[:, 0:512], wslot_gu(ui)[:, fcx, ic * 128:(ic + 1) * 128],
                                               xn3[:, fcx, sub * 512:(sub + 1) * 512],
                                               start=(fcx == 0), stop=(fcx == FC - 1))
                            if fcx == FC - 1:
                                em.inc(mm, "s_wb" if last_mat else "s_t")
                    t_u = em.mark("s_t")
                    wb_u = em.mark("s_wb")
                    @block.vector
                    def _(vector, ic=ic, sub=sub, bank=bank, t_u=t_u, wb_u=wb_u, last_mat=last_mat):
                        if last_mat:
                            em.wait(vector, "s_wb", wb_u)
                        else:
                            em.wait(vector, "s_t", t_u)
                        em.wait(vector, "s_s")
                        em.inc(vector.tensor_tensor(h3[:, ic, sub * 512:(sub + 1) * 512],
                                                    h3[:, ic, sub * 512:(sub + 1) * 512], bank[:, 0:512], ALU.mult), "s_v")
                    vmarks.append(em.mark("s_v"))
            wcons[0] += 1
            di = wcons[0]
            emarks = []
            for sub in range(4):
                @block.tensor
                def _(tensor, sub=sub, e=e, di=di):
                    if sub == 0:
                        em.wait(tensor, "s_wa", wmarks[di])
                    em.wait(tensor, "s_v")
                    em.inc(tensor.matmul(PS[6][:, 0:512], oneh[0:16, e * 128:(e + 1) * 128],
                                         comb[0:16, sub * 512:(sub + 1) * 512], start=True, stop=True), "s_t")
                t_cb = em.mark("s_t")
                @block.vector
                def _(vector, t_cb=t_cb):
                    em.wait(vector, "s_t", t_cb)
                    em.inc(vector.tensor_copy(combb[:], PS[6][:, 0:512]), "s_v")
                for hc in range(FC):
                    n = sub * FC + hc
                    last_mat = (n == 4 * FC - 1)
                    bank = PS[4 + n % 2]
                    @block.tensor
                    def _(tensor, n=n, hc=hc, sub=sub, bank=bank, di=di, last_mat=last_mat):
                        if n >= 2:
                            em.wait(tensor, "s_v", emarks[n - 2])
                        for ic in range(IC):
                            mm = tensor.matmul(bank[:, 0:512], wslot_dn(di)[:, ic, hc * 128:(hc + 1) * 128],
                                               h3[:, ic, sub * 512:(sub + 1) * 512],
                                               start=(ic == 0), stop=(ic == IC - 1))
                            if ic == IC - 1:
                                em.inc(mm, "s_wb" if last_mat else "s_t")
                    t_d = em.mark("s_t")
                    wb_d = em.mark("s_wb")
                    @block.vector
                    def _(vector, hc=hc, sub=sub, bank=bank, t_d=t_d, wb_d=wb_d, last_mat=last_mat):
                        if last_mat:
                            em.wait(vector, "s_wb", wb_d)
                        else:
                            em.wait(vector, "s_t", t_d)
                        em.wait(vector, "s_v")
                        em.wait(vector, "s_v", em.inc(vector.tensor_tensor(tmp2[:], bank[:, 0:512], combb[:], ALU.mult), "s_v"))
                        em.inc(vector.tensor_tensor(acc3[:, hc, sub * 512:(sub + 1) * 512],
                                                    acc3[:, hc, sub * 512:(sub + 1) * 512], tmp2[:], ALU.add), "s_v")
                    emarks.append(em.mark("s_v"))
            wcons[0] += 1

        def emit_ffn_supertile(writeback_base=None):
            for sub in range(4):
                emit_ln2(acc3[:, :, sub * 512:(sub + 1) * 512], xn3[:, :, sub * 512:(sub + 1) * 512])
                emit_router(sub)
            for e in range(9):
                emit_expert(e)
            if writeback_base is not None:
                for sub in range(4):
                    emit_ln2(acc3[:, :, sub * 512:(sub + 1) * 512], xn3[:, :, sub * 512:(sub + 1) * 512])
                @block.sync
                def _(sync, g0=writeback_base):
                    em.wait(sync, "s_m")
                    em.wait(sync, "s_v")
                    em.dinc(sync.dma_start(out=x3f[:, :, g0:g0 + 2048], in_=acc3), "s_m")
                    em.dinc(sync.dma_start(out=x3nf[:, :, g0:g0 + 2048], in_=xn3), "s_m")

        if UPTO >= 3:
            for g in range(2):
                @block.sync
                def _(sync, g=g):
                    em.wait(sync, "s_m")
                    em.wait(sync, "s_xa")
                    em.wait(sync, "s_xb")
                    em.wait(sync, "s_v")
                    em.wait(sync, "s_s")
                    em.wait(sync, "s_t")
                    em.dinc(sync.dma_start(out=acc3, in_=x2f[:, :, g * 2048:(g + 1) * 2048]), "s_m")
                emit_ffn_supertile(writeback_base=g * 2048)

        if UPTO >= 4:
            HARf = HAR.ap().bitcast(F32)
            ARN2f = ARN2.ap().bitcast(F32)

            def xnb(k):
                return HAR[:, (k % 2) * 4096:(k % 2) * 4096 + 4096].rearrange("p (c t) -> p c t", c=FC)
            stg = [HAR[:, 8192 + i * 512:8192 + (i + 1) * 512] for i in range(2)]
            S_sb = HAR[:, 9216:11264].rearrange("p (c j) -> p c j", j=64)
            KT4 = HAR[:, 11264:11776].rearrange("p (q d) -> p q d", q=4)
            LmT_acc = HARf[0:64, 10240:11264].rearrange("p (c d) -> p c d", c=FC)
            Z_acc = HARf[0:64, 2048:3072].rearrange("p (h j) -> p h j", h=16)
            lidxb = HARf[:, 5888:5952]
            pc_col = HARf[:, 5952:5953]
            tcol_f = HARf[:, 5953:5954]
            Y_bf = HAR[0:64, 21408:22432].rearrange("p (h d) -> p h d", h=16)
            w_bf = HAR[0:64, 22432:22448]
            Lm_all = HAR[0:64, 13312:14336].rearrange("p (h j) -> p h j", h=16)
            Lm2_all = HAR[0:64, 14336:15360].rearrange("p (h j) -> p h j", h=16)
            LmT_sb = HAR[:, 16672:17696].rearrange("p (c d) -> p c d", c=FC)
            Ln_sb = HARf[0:64, 7680:7696]
            Ksq_sb = HAR[:, 15392:15904]
            Phi_sb = HAR[0:64, 15904:16416]
            Phi_s2 = HAR[0:64, 6144:6656]
            PhiT_s2 = HAR[:, 6656:6912].rearrange("p (q j) -> p q j", q=4)
            PhiT_sb = HAR[:, 16416:16672].rearrange("p (q j) -> p q j", q=4)
            VT2 = HAR[:, 16672:20768].rearrange("p (q c d) -> p q c d", q=4, c=FC)
            Klo = HAR[0:64, 12288:12800]
            Ksqlo = HAR[0:64, 12800:13312]
            ideps = HARf[0:64, 8080:8144]
            twoI8 = HARf[0:64, 8144:8656]
            W_all = HARf[0:64, 8656:9680].rearrange("p (h j) -> p h j", h=16)
            X_all = HARf[0:64, 9680:10704].rearrange("p (h j) -> p h j", h=16)
            T2_sb = HARf[0:64, 5888:6400]
            Knb_B = HARf[0:64, 10384:10896]
            Knb_C = HARf[0:64, 4104:4616]
            rnb_sb = HARf[0:64, 4616:5128]
            ZU_all = ARN2f[0:64, 0:1040].rearrange("p (h j) -> p h j", h=16)
            Yu_all = ARN2f[0:64, 1040:2080].rearrange("p (h j) -> p h j", h=16)
            AO_fm = HAR[:, 0:4096].rearrange("p (c t) -> p c t", c=FC)
            Q_fm = ARN2.ap().rearrange("p (c t) -> p c t", c=FC)

            @block.vector
            def _(vector):
                em.wait(vector, "s_m", lidx_ld)
                em.wait(vector, "s_g", p3_g)
                em.wait(vector, "s_v")
                em.wait(vector, "s_v", em.inc(vector.tensor_copy(ROWS[0:1, 0:64], iotat[0:1, 0:64]), "s_v"))
                em.wait(vector, "s_v", em.inc(vector.tensor_scalar(ROWS[0:1, 0:64], ROWS[0:1, 0:64], 0.0, 4095.0, ALU.max, ALU.min), "s_v"))
                em.inc(vector.tensor_copy(tcol_f[:], iotat[:, 64:65]), "s_v")
            p3_v1 = em.mark("s_v")
            @block.tensor
            def _(tensor):
                em.wait(tensor, "s_v", p3_v1)
                em.inc(tensor.matmul(PS[6][:, 0:64], ones_f[0:1, 0:128], ROWS[0:1, 0:64], start=True, stop=True), "s_t")
            p3_t1 = em.mark("s_t")
            @block.vector
            def _(vector):
                em.wait(vector, "s_t", p3_t1)
                em.wait(vector, "s_v", em.inc(vector.tensor_copy(lidxb[:], PS[6][:, 0:64]), "s_v"))
                for c in range(32):
                    em.wait(vector, "s_v", em.inc(vector.tensor_scalar(pc_col[:], tcol_f[:], float(c * 128), None, ALU.add), "s_v"))
                    em.wait(vector, "s_v", em.inc(vector.tensor_scalar(S_sb[:, c, :], lidxb[:], pc_col[:], None, ALU.is_equal), "s_v"))

            stg2 = HAR[:, 12288:20480].rearrange("p (b c t) -> p b c t", b=2, c=FC)

            def emit_proj_pass(widx, bia_c, dst_dram, gather):
                st_marks = []
                ev_marks = []
                for tt in range(NT):
                    @block.sync
                    def _(sync, tt=tt):
                        em.wait(sync, "s_m")
                        em.wait(sync, "s_v")
                        em.dinc(sync.dma_start(out=xnb(tt), in_=x3nf[:, :, tt * 512:(tt + 1) * 512]), "s_m")
                    ld_m = em.mark("s_m")
                    for oc in range(FC):
                        n = tt * FC + oc
                        last_mat = (n == NT * FC - 1)
                        bank = PS[n % 4]
                        @block.tensor
                        def _(tensor, tt=tt, oc=oc, n=n, bank=bank, widx=widx, ld_m=ld_m, last_mat=last_mat):
                            em.wait(tensor, "s_m", ld_m)
                            if n == 0:
                                em.wait(tensor, "s_wa", wmarks[widx])
                                em.wait(tensor, "s_v")
                                em.wait(tensor, "s_s")
                            if n >= 4:
                                em.wait(tensor, "s_v", ev_marks[n - 4])
                            for fcx in range(FC):
                                mm = tensor.matmul(bank[:, 0:512], wslot_ab(widx)[:, fcx, oc * 128:(oc + 1) * 128],
                                                   xnb(tt)[:, fcx, :], start=(fcx == 0), stop=(fcx == FC - 1))
                                if fcx == FC - 1:
                                    em.inc(mm, "s_wb" if last_mat else "s_t")
                        t_p = em.mark("s_t")
                        wb_p = em.mark("s_wb")
                        @block.vector
                        def _(vector, tt=tt, oc=oc, n=n, bank=bank, bia_c=bia_c, t_p=t_p, wb_p=wb_p, last_mat=last_mat):
                            if last_mat:
                                em.wait(vector, "s_wb", wb_p)
                            else:
                                em.wait(vector, "s_t", t_p)
                            if oc == 0 and len(st_marks) >= 2:
                                em.wait(vector, "s_m", st_marks[-2])
                            em.inc(vector.tensor_scalar(stg2[:, tt % 2, oc, :], bank[:, 0:512],
                                                        bia[:, bia_c + oc:bia_c + oc + 1], None, ALU.add), "s_v")
                        ev_marks.append(em.mark("s_v"))
                        if gather:
                            v_e = ev_marks[-1]
                            for q in range(4):
                                @block.tensor
                                def _(tensor, tt=tt, oc=oc, q=q, v_e=v_e):
                                    em.wait(tensor, "s_v", v_e)
                                    em.wait(tensor, "s_v")
                                    em.inc(tensor.transpose(PS[6].ap().bitcast(BF16)[:, 0:128],
                                                            stg2[:, tt % 2, oc, q * 128:(q + 1) * 128], id_bf[:]), "s_t")
                                t_tr = em.mark("s_t")
                                @block.vector
                                def _(vector, q=q, t_tr=t_tr):
                                    em.wait(vector, "s_t", t_tr)
                                    em.inc(vector.tensor_copy(KT4[:, q, :], PS[6].ap().bitcast(BF16)[:, 0:128]), "s_v")
                            v_kt = em.mark("s_v")
                            @block.tensor
                            def _(tensor, tt=tt, oc=oc, v_kt=v_kt):
                                em.wait(tensor, "s_v", v_kt)
                                for q in range(4):
                                    em.inc(tensor.matmul(PS[5][0:64, 0:128], S_sb[:, tt * 4 + q, :], KT4[:, q, :],
                                                         start=(q == 0), stop=(q == 3)), "s_t")
                            t_ga = em.mark("s_t")
                            @block.vector
                            def _(vector, oc=oc, t_ga=t_ga):
                                em.wait(vector, "s_t", t_ga)
                                em.wait(vector, "s_v")
                                em.inc(vector.tensor_tensor(LmT_acc[:, oc, :], LmT_acc[:, oc, :], PS[5][0:64, 0:128],
                                                            ALU.add), "s_v")
                            ev_marks[-1] = em.mark("s_v")
                    @block.sync
                    def _(sync, tt=tt):
                        em.wait(sync, "s_m")
                        em.wait(sync, "s_v")
                        em.dinc(sync.dma_start(out=dst_dram[:, :, tt * 512:(tt + 1) * 512], in_=stg2[:, tt % 2, :, :]), "s_m")
                    st_marks.append(em.mark("s_m"))

            wk_i = n_ffn1_mats
            emit_proj_pass(wk_i, 8, Kf, gather=True)
            emit_proj_pass(wk_i + 1, 16, Vf, gather=False)
            emit_proj_pass(wk_i + 2, 0, Qf, gather=False)
            wcons[0] = wk_i + 3

            for oc in range(FC):
                @block.tensor
                def _(tensor, oc=oc):
                    em.wait(tensor, "s_v")
                    em.inc(tensor.transpose(PS[6][:, 0:64], LmT_acc[:, oc, :], idf[0:64, 0:64]), "s_t")
                t_lm = em.mark("s_t")
                @block.vector
                def _(vector, oc=oc, t_lm=t_lm):
                    em.wait(vector, "s_t", t_lm)
                    em.wait(vector, "s_v", em.inc(vector.tensor_copy(Lm_all[:, 2 * oc, :], PS[6][0:64, 0:64]), "s_v"))
                    em.inc(vector.tensor_copy(Lm_all[:, 2 * oc + 1, :], PS[6][64:128, 0:64]), "s_v")
            p3_lm0 = em.mark("s_v")
            @block.vector
            def _(vector):
                em.wait(vector, "s_v", p3_lm0)
                em.inc(vector.tensor_scalar(Lm2_all.rearrange("p h j -> p (h j)"), Lm_all.rearrange("p h j -> p (h j)"),
                                            -2.0, None, ALU.mult), "s_v")
            p3_lm = em.mark("s_v")

            for h in range(NH):
                @block.scalar
                def _(scalar, h=h):
                    em.wait(scalar, "s_v", p3_lm)
                    em.wait(scalar, "s_t")
                    em.inc(scalar.activation(ROWS[0:64, 0:64], Lm_all[:, h, :], AF.Square), "s_s")
                s_lsq = em.mark("s_s")
                @block.tensor
                def _(tensor, h=h, s_lsq=s_lsq):
                    em.wait(tensor, "s_s", s_lsq)
                    em.wait(tensor, "s_v")
                    em.inc(tensor.matmul(PS[6][0:1, 0:64], ones_f[0:64, 0:1], ROWS[0:64, 0:64],
                                         start=True, stop=True), "s_t")
                t_ln = em.mark("s_t")
                @block.vector
                def _(vector, t_ln=t_ln):
                    em.wait(vector, "s_t", t_ln)
                    em.inc(vector.tensor_copy(ROWS[0:1, 64:128], PS[6][0:1, 0:64]), "s_v")
                v_lnr = em.mark("s_v")
                @block.tensor
                def _(tensor, v_lnr=v_lnr):
                    em.wait(tensor, "s_v", v_lnr)
                    em.inc(tensor.transpose(PS[6][0:64, 64:65], ROWS[0:1, 64:128], idf[0:1, 0:1]), "s_t")
                t_lnc = em.mark("s_t")
                @block.vector
                def _(vector, h=h, t_lnc=t_lnc):
                    em.wait(vector, "s_t", t_lnc)
                    em.inc(vector.tensor_copy(Ln_sb[:, h:h + 1], PS[6][0:64, 64:65]), "s_v")

            Vtile = HAR[:, 8192:12288].rearrange("p (c t) -> p c t", c=FC)
            @block.vector
            def _(vector):
                em.wait(vector, "s_t")
                em.inc(vector.memset(ROWS[0:64, 1024:1040], 0.0), "s_v")
                em.inc(vector.memset(Z_acc.rearrange("p h j -> p (h j)"), 0.0), "s_v")
            Kt_B = HAR[:, 0:4096].rearrange("p (c t) -> p c t", c=FC)
            ts_marks = {}
            phi_marks = {}
            pt_marks = {}
            z_marks = []
            ksq_marks = {}
            for tt in range(NT):
                @block.sync
                def _(sync, tt=tt):
                    em.wait(sync, "s_m")
                    em.wait(sync, "s_v")
                    em.wait(sync, "s_s")
                    em.wait(sync, "s_t")
                    em.dinc(sync.dma_start(out=Kt_B, in_=Kf[:, :, tt * 512:(tt + 1) * 512]), "s_m")
                    em.dinc(sync.dma_start(out=Vtile.rearrange("p c t -> p (c t)"),
                                           in_=Vf[:, :, tt * 512:(tt + 1) * 512]), "s_m")
                ld_b = em.mark("s_m")
                vt_marks = []
                for oc in range(FC):
                    for q in range(4):
                        @block.tensor
                        def _(tensor, oc=oc, q=q, ld_b=ld_b):
                            em.wait(tensor, "s_m", ld_b)
                            if vt_marks:
                                em.wait(tensor, "s_v", vt_marks[-1])
                            elif z_marks:
                                em.wait(tensor, "s_v", z_marks[-1])
                            em.inc(tensor.transpose(PS[7].ap().bitcast(BF16)[:, 0:128],
                                                    Vtile[:, oc, q * 128:(q + 1) * 128], id_bf[:]), "s_t")
                        t_vt = em.mark("s_t")
                        @block.vector
                        def _(vector, oc=oc, q=q, t_vt=t_vt):
                            em.wait(vector, "s_t", t_vt)
                            em.inc(vector.tensor_copy(VT2[:, q, oc, :], PS[7].ap().bitcast(BF16)[:, 0:128]), "s_v")
                        vt_marks.append(em.mark("s_v"))
                for h in range(NH):
                    oc, par = h // 2, h % 2
                    Phi_b = Phi_sb if par == 0 else Phi_s2
                    PhiT_b = PhiT_sb if par == 0 else PhiT_s2
                    tsreg = ROWS[0:64, 512:1024] if par == 0 else ROWS[0:64, 0:512]
                    gh = tt * NH + h
                    if par == 0:
                        @block.scalar
                        def _(scalar, tt=tt, oc=oc, gh=gh):
                            em.wait(scalar, "s_m", ld_b)
                            if gh - 1 in ts_marks:
                                em.wait(scalar, "s_v", ts_marks[gh - 1])
                            em.inc(scalar.activation(Ksq_sb[:], Kt_B[:, oc, :], AF.Square), "s_s")
                        ksq_marks[gh] = em.mark("s_s")
                        @block.vector
                        def _(vector, gh=gh, oc=oc):
                            em.wait(vector, "s_s", ksq_marks[gh])
                            if gh - 1 in ts_marks:
                                em.wait(vector, "s_v", ts_marks[gh - 1])
                            em.inc(vector.tensor_copy(Klo[:], Kt_B[64:128, oc, :]), "s_v")
                            em.inc(vector.tensor_copy(Ksqlo[:], Ksq_sb[64:128, :]), "s_v")
                    s_ksq = ksq_marks[gh - par]
                    v_shift = em.mark("s_v")
                    rK = (lambda oc=oc: Kt_B[0:64, oc, :]) if par == 0 else (lambda: Klo[:])
                    rS = (lambda: Ksq_sb[0:64, :]) if par == 0 else (lambda: Ksqlo[:])
                    @block.tensor
                    def _(tensor, h=h, gh=gh, par=par, rK=rK, rS=rS, s_ksq=s_ksq, v_shift=v_shift):
                        em.wait(tensor, "s_s", s_ksq)
                        em.wait(tensor, "s_v", v_shift)
                        if gh - 2 in ts_marks:
                            em.wait(tensor, "s_v", ts_marks[gh - 2])
                        tensor.matmul(PS[par][0:64, 0:512], Lm2_all[:, h, :], rK(), start=True, stop=False)
                        em.inc(tensor.matmul(PS[par][0:64, 0:512], ones64[:], rS(), start=False, stop=True), "s_t")
                    t_xl = em.mark("s_t")
                    @block.vector
                    def _(vector, par=par, gh=gh, t_xl=t_xl, tsreg=tsreg):
                        em.wait(vector, "s_t", t_xl)
                        if gh - 2 in phi_marks:
                            em.wait(vector, "s_s", phi_marks[gh - 2])
                        em.inc(vector.tensor_scalar(tsreg, PS[par][0:64, 0:512],
                                                    Ln_sb[:, h:h + 1], 0.0, ALU.add, ALU.max), "s_v")
                    ts_marks[gh] = em.mark("s_v")
                    @block.scalar
                    def _(scalar, gh=gh, tsreg=tsreg, Phi_b=Phi_b):
                        em.wait(scalar, "s_v", ts_marks[gh])
                        if gh - 2 in pt_marks:
                            em.wait(scalar, "s_t", pt_marks[gh - 2])
                        em.inc(scalar.activation(Phi_b[:], tsreg, AF.Exp, scale=scb[0:64, 2:3]), "s_s")
                    phi_marks[gh] = em.mark("s_s")
                    @block.vector
                    def _(vector, h=h, gh=gh, Phi_b=Phi_b):
                        em.wait(vector, "s_s", phi_marks[gh])
                        em.wait(vector, "s_v", em.inc(vector.reduce_sum(tb[0:64, 60:61], Phi_b[:], axis=AX.X), "s_v"))
                        em.inc(vector.tensor_tensor(ROWS[0:64, 1024 + h:1025 + h], ROWS[0:64, 1024 + h:1025 + h],
                                                    tb[0:64, 60:61], ALU.add), "s_v")
                    ptc = []
                    for q in range(4):
                        @block.tensor
                        def _(tensor, gh=gh, q=q, Phi_b=Phi_b):
                            em.wait(tensor, "s_s", phi_marks[gh])
                            if ptc:
                                em.wait(tensor, "s_v", ptc[-1])
                            em.inc(tensor.transpose(PS[7].ap().bitcast(BF16)[:, 0:64],
                                                    Phi_b[:, q * 128:(q + 1) * 128], id_bf[0:64, 0:64]), "s_t")
                        t_pt = em.mark("s_t")
                        @block.vector
                        def _(vector, q=q, t_pt=t_pt, PhiT_b=PhiT_b):
                            em.wait(vector, "s_t", t_pt)
                            em.inc(vector.tensor_copy(PhiT_b[:, q, :], PS[7].ap().bitcast(BF16)[:, 0:64]), "s_v")
                        ptc.append(em.mark("s_v"))
                    pt_marks[gh] = em.mark("s_t")
                    v_pt = em.mark("s_v")
                    @block.tensor
                    def _(tensor, h=h, oc=oc, par=par, v_pt=v_pt, PhiT_b=PhiT_b):
                        em.wait(tensor, "s_v", v_pt)
                        if z_marks:
                            em.wait(tensor, "s_v", z_marks[-1])
                        for q in range(4):
                            em.inc(tensor.matmul(PS[5][0:64, 0:64], PhiT_b[:, q, :],
                                                 VT2[:, q, oc, par * 64:(par + 1) * 64],
                                                 start=(q == 0), stop=(q == 3)), "s_t")
                    t_z = em.mark("s_t")
                    @block.vector
                    def _(vector, h=h, t_z=t_z):
                        em.wait(vector, "s_t", t_z)
                        em.inc(vector.tensor_tensor(Z_acc[:, h, :], Z_acc[:, h, :], PS[5][0:64, 0:64], ALU.add), "s_v")
                    z_marks.append(em.mark("s_v"))

            @block.vector
            def _(vector):
                em.wait(vector, "s_t")
                em.inc(vector.tensor_scalar(ideps[:], idf[0:64, 0:64], 1e-6, None, ALU.mult), "s_v")
                for hh in range(8):
                    em.inc(vector.tensor_scalar(twoI8[:, hh * 64:(hh + 1) * 64], idf[0:64, 0:64], 2.0, None, ALU.mult), "s_v")
            p3_c = em.mark("s_v")
            for h in range(NH):
                oc, par = h // 2, h % 2
                p0 = 64 * par
                @block.tensor
                def _(tensor, h=h):
                    em.wait(tensor, "s_v")
                    em.inc(tensor.transpose(PS[6][0:1, 0:64], Ln_sb[:, h:h + 1], idf[0:64, 0:64]), "s_t")
                t_lr = em.mark("s_t")
                @block.vector
                def _(vector, t_lr=t_lr):
                    em.wait(vector, "s_t", t_lr)
                    em.inc(vector.tensor_copy(ROWS[0:1, 64:128], PS[6][0:1, 0:64]), "s_v")
                v_lr = em.mark("s_v")
                @block.tensor
                def _(tensor, v_lr=v_lr):
                    em.wait(tensor, "s_v", v_lr)
                    em.inc(tensor.matmul(PS[6][0:64, 128:192], ones_f[0:1, 0:64], ROWS[0:1, 64:128], start=True, stop=True), "s_t")
                t_lb = em.mark("s_t")
                @block.vector
                def _(vector, t_lb=t_lb):
                    em.wait(vector, "s_t", t_lb)
                    em.inc(vector.tensor_copy(Knb_B[:, 0:64], PS[6][0:64, 128:192]), "s_v")
                v_lb = em.mark("s_v")
                bank = PS[par]
                @block.tensor
                def _(tensor, h=h, bank=bank, v_lb=v_lb):
                    em.wait(tensor, "s_v", v_lb)
                    em.inc(tensor.matmul(bank[0:64, 0:64], Lm2_all[:, h, :], Lm_all[:, h, :],
                                         start=True, stop=True), "s_t")
                t_xlw = em.mark("s_t")
                @block.vector
                def _(vector, h=h, bank=bank, t_xlw=t_xlw):
                    em.wait(vector, "s_t", t_xlw)
                    em.wait(vector, "s_v", em.inc(vector.scalar_tensor_tensor(ROWS[0:64, 128:192], bank[0:64, 0:64],
                                                                              Ln_sb[:, h:h + 1], Knb_B[:, 0:64],
                                                                              ALU.add, ALU.add), "s_v"))
                    em.inc(vector.tensor_scalar(ROWS[0:64, 192:256], ROWS[0:64, 128:192], 0.0, scb[0:64, 2:3],
                                                ALU.max, ALU.mult), "s_v")
                v_wexp = em.mark("s_v")
                @block.scalar
                def _(scalar, v_wexp=v_wexp):
                    em.wait(scalar, "s_v", v_wexp)
                    em.inc(scalar.activation(ROWS[0:64, 128:192], ROWS[0:64, 192:256], AF.Exp), "s_s")
                s_wx = em.mark("s_s")
                @block.vector
                def _(vector, h=h, s_wx=s_wx):
                    em.wait(vector, "s_s", s_wx)
                    em.wait(vector, "s_v", p3_c)
                    em.inc(vector.tensor_tensor(W_all[:, h, :], ROWS[0:64, 128:192], ideps[:], ALU.add), "s_v")
            p3_w = em.mark("s_v")

            @block.vector
            def _(vector):
                em.wait(vector, "s_v", p3_w)
                em.inc(vector.tensor_scalar(X_all.rearrange("p h j -> p (h j)"), W_all.rearrange("p h j -> p (h j)"),
                                            1.0 / 4096.0, None, ALU.mult), "s_v")
            ns_init = em.mark("s_v")

            NS_IT = 26
            for it in range(NS_IT):
                for bh in range(2):
                    hs = range(bh * 8, bh * 8 + 8)
                    @block.tensor
                    def _(tensor, hs=hs):
                        em.wait(tensor, "s_v")
                        for h in hs:
                            em.inc(tensor.matmul(PS[bh][0:64, (h % 8) * 64:(h % 8 + 1) * 64],
                                                 W_all[:, h, :], X_all[:, h, :], start=True, stop=True), "s_t")
                    t_t1 = em.mark("s_t")
                    @block.vector
                    def _(vector, t_t1=t_t1):
                        em.wait(vector, "s_t", t_t1)
                        em.inc(vector.scalar_tensor_tensor(T2_sb[:], PS[bh][0:64, 0:512], -1.0, twoI8[:],
                                                           ALU.mult, ALU.add), "s_v")
                    v_t2 = em.mark("s_v")
                    @block.tensor
                    def _(tensor, hs=hs, v_t2=v_t2):
                        em.wait(tensor, "s_v", v_t2)
                        for h in hs:
                            em.inc(tensor.matmul(PS[2 + bh][0:64, (h % 8) * 64:(h % 8 + 1) * 64],
                                                 X_all[:, h, :], T2_sb[:, (h % 8) * 64:(h % 8 + 1) * 64],
                                                 start=True, stop=True), "s_t")
                    t_x2 = em.mark("s_t")
                    @block.vector
                    def _(vector, bh=bh, t_x2=t_x2):
                        em.wait(vector, "s_t", t_x2)
                        em.inc(vector.tensor_copy(X_all[:, bh * 8:(bh + 1) * 8, :].rearrange("p h j -> p (h j)"),
                                                  PS[2 + bh][0:64, 0:512]), "s_v")
            ns_done = em.mark("s_v")

            @block.vector
            def _(vector):
                em.wait(vector, "s_t")
                for h in range(NH):
                    em.inc(vector.tensor_copy(ZU_all[:, h, 0:64], Z_acc[:, h, :]), "s_v")
                    em.inc(vector.tensor_copy(ZU_all[:, h, 64:65], ROWS[0:64, 1024 + h:1025 + h]), "s_v")
            zu_v = em.mark("s_v")
            for h in range(NH):
                bank = PS[h % 2]
                @block.tensor
                def _(tensor, h=h, bank=bank):
                    em.wait(tensor, "s_v")
                    em.inc(tensor.matmul(bank[0:64, 0:65], X_all[:, h, :], ZU_all[:, h, 0:65], start=True, stop=True), "s_t")
                t_y0 = em.mark("s_t")
                @block.vector
                def _(vector, h=h, bank=bank, t_y0=t_y0):
                    em.wait(vector, "s_t", t_y0)
                    em.inc(vector.tensor_copy(Yu_all[:, h, 0:65], bank[0:64, 0:65]), "s_v")
                v_y0 = em.mark("s_v")
                if not REFINE_FLAG:
                    @block.vector
                    def _(vector, h=h, v_y0=v_y0):
                        em.wait(vector, "s_v", v_y0)
                        em.inc(vector.tensor_copy(Y_bf[:, h, :], Yu_all[:, h, 0:64]), "s_v")
                        em.inc(vector.tensor_copy(w_bf[:, h:h + 1], Yu_all[:, h, 64:65]), "s_v")
                    continue
                @block.tensor
                def _(tensor, h=h, bank=bank, v_y0=v_y0):
                    em.wait(tensor, "s_v", v_y0)
                    em.inc(tensor.matmul(bank[0:64, 128:193], W_all[:, h, :], Yu_all[:, h, 0:65], start=True, stop=True), "s_t")
                t_wy = em.mark("s_t")
                @block.vector
                def _(vector, h=h, bank=bank, t_wy=t_wy):
                    em.wait(vector, "s_t", t_wy)
                    em.inc(vector.scalar_tensor_tensor(ROWS[0:64, 0:65], bank[0:64, 128:193], -1.0,
                                                       ZU_all[:, h, 0:65], ALU.mult, ALU.add), "s_v")
                v_rr2 = em.mark("s_v")
                @block.tensor
                def _(tensor, h=h, bank=bank, v_rr2=v_rr2):
                    em.wait(tensor, "s_v", v_rr2)
                    em.inc(tensor.matmul(bank[0:64, 256:321], X_all[:, h, :], ROWS[0:64, 0:65], start=True, stop=True), "s_t")
                t_dy = em.mark("s_t")
                @block.vector
                def _(vector, h=h, bank=bank, t_dy=t_dy):
                    em.wait(vector, "s_t", t_dy)
                    em.wait(vector, "s_v", em.inc(vector.tensor_tensor(Yu_all[:, h, 0:65], Yu_all[:, h, 0:65],
                                                                       bank[0:64, 256:321], ALU.add), "s_v"))
                    em.inc(vector.tensor_copy(Y_bf[:, h, :], Yu_all[:, h, 0:64]), "s_v")
                    em.inc(vector.tensor_copy(w_bf[:, h:h + 1], Yu_all[:, h, 64:65]), "s_v")
            solve_done = em.mark("s_v")

            @block.gpsimd
            def _(gpsimd):
                em.wait(gpsimd, "s_m")
                em.wait(gpsimd, "s_v", solve_done)
                em.wait(gpsimd, "s_wa")
                pid = gpsimd.partition_id()
                off = gpsimd.compute_val((pid % 2) * 2048)
                em.dinc(gpsimd.dma_start(out=Q_fm.rearrange("p c t -> p (c t)"),
                                         in_=Qf[:, :, bass.DynSlice(off, 2048)]), "s_wa")
                em.dinc(gpsimd.dma_start(out=acc3.rearrange("p c t -> p (c t)"),
                                         in_=x3f[:, :, bass.DynSlice(off, 2048)]), "s_wa")
            q_ld = em.mark("s_wa")
            if len(wstream) > n_ffn1_mats + 5:
                emit_wloads(n_ffn1_mats + 5, len(wstream))
            for ct in range(4):
                tsl = slice(ct * 512, (ct + 1) * 512)
                for h in range(NH):
                    oc, par = h // 2, h % 2
                    if par == 0:
                        @block.scalar
                        def _(scalar, ct=ct, oc=oc):
                            em.wait(scalar, "s_wa", q_ld)
                            em.wait(scalar, "s_t")
                            em.inc(scalar.activation(Ksq_sb[:], Q_fm[:, oc, tsl], AF.Square), "s_s")
                        s_qsq0 = em.mark("s_s")
                        @block.vector
                        def _(vector, ct=ct, oc=oc, s_qsq0=s_qsq0):
                            em.wait(vector, "s_s", s_qsq0)
                            em.wait(vector, "s_t")
                            em.inc(vector.tensor_copy(Klo[:], Q_fm[64:128, oc, tsl]), "s_v")
                            em.inc(vector.tensor_copy(Ksqlo[:], Ksq_sb[64:128, :]), "s_v")
                    s_qsq = em.mark("s_s")
                    rQ = (lambda oc=oc, tsl=tsl: Q_fm[0:64, oc, tsl]) if par == 0 else (lambda: Klo[:])
                    rSq = (lambda: Ksq_sb[0:64, :]) if par == 0 else (lambda: Ksqlo[:])
                    @block.tensor
                    def _(tensor, h=h, par=par, rQ=rQ, rSq=rSq, s_qsq=s_qsq):
                        em.wait(tensor, "s_wa", q_ld)
                        em.wait(tensor, "s_s", s_qsq)
                        em.wait(tensor, "s_v")
                        tensor.matmul(PS[par][0:64, 0:512], Lm2_all[:, h, :], rQ(), start=True, stop=False)
                        em.inc(tensor.matmul(PS[par][0:64, 0:512], ones64[:], rSq(), start=False, stop=True), "s_t")
                    t_xq = em.mark("s_t")
                    @block.vector
                    def _(vector, h=h, par=par, t_xq=t_xq):
                        em.wait(vector, "s_t", t_xq)
                        em.inc(vector.tensor_scalar(ROWS[0:64, 512:1024], PS[par][0:64, 0:512],
                                                    Ln_sb[:, h:h + 1], 0.0, ALU.add, ALU.max), "s_v")
                    v_phq = em.mark("s_v")
                    @block.scalar
                    def _(scalar, v_phq=v_phq):
                        em.wait(scalar, "s_v", v_phq)
                        em.wait(scalar, "s_t")
                        em.inc(scalar.activation(Phi_sb[:], ROWS[0:64, 512:1024], AF.Exp, scale=scb[0:64, 2:3]), "s_s")
                    s_phq = em.mark("s_s")
                    @block.tensor
                    def _(tensor, h=h, s_phq=s_phq):
                        em.wait(tensor, "s_s", s_phq)
                        em.wait(tensor, "s_v")
                        em.inc(tensor.matmul(PS[4][0:64, 0:512], Y_bf[:, h, :], Phi_sb[:], start=True, stop=True), "s_t")
                        em.inc(tensor.matmul(PS[5][0:1, 0:512], w_bf[:, h:h + 1], Phi_sb[:], start=True, stop=True), "s_t")
                    t_ou = em.mark("s_t")
                    @block.vector
                    def _(vector, t_ou=t_ou):
                        em.wait(vector, "s_t", t_ou)
                        em.wait(vector, "s_v", em.inc(vector.tensor_scalar(ROWS[0:1, 1040:1552], PS[5][0:1, 0:512],
                                                                           1e-10, None, ALU.max), "s_v"))
                        em.inc(vector.reciprocal(ROWS[0:1, 1552:2064] if False else ROWS[0:1, 0:512], ROWS[0:1, 1040:1552]), "s_v")
                    v_nr = em.mark("s_v")
                    @block.tensor
                    def _(tensor, v_nr=v_nr):
                        em.wait(tensor, "s_v", v_nr)
                        em.inc(tensor.matmul(PS[6][0:64, 0:512], ones_f[0:1, 0:64], ROWS[0:1, 0:512],
                                             start=True, stop=True), "s_t")
                    t_nrb = em.mark("s_t")
                    @block.vector
                    def _(vector, oc=oc, p0=64 * par, t_nrb=t_nrb):
                        em.wait(vector, "s_t", t_nrb)
                        em.wait(vector, "s_v", em.inc(vector.tensor_copy(rnb_sb[:, 0:512], PS[6][0:64, 0:512]), "s_v"))
                        em.inc(vector.tensor_tensor(AO_fm[p0:p0 + 64, oc, :], PS[4][0:64, 0:512], rnb_sb[:, 0:512],
                                                    ALU.mult), "s_v")
                ao_done = em.mark("s_v")
                wo_i = wk_i + 3
                for oc in range(FC):
                    n2 = ct * FC + oc
                    last_mat = (n2 == 4 * FC - 1)
                    bank = PS[n2 % 2]
                    @block.tensor
                    def _(tensor, ct=ct, oc=oc, bank=bank, wo_i=wo_i, ao_done=ao_done, n2=n2, last_mat=last_mat):
                        em.wait(tensor, "s_v", ao_done)
                        if n2 % 8 >= 2 or ct > 0:
                            em.wait(tensor, "s_v")
                        if n2 == 0:
                            em.wait(tensor, "s_wa", wmarks[wo_i])
                        for ic in range(FC):
                            mm = tensor.matmul(bank[:, 0:512], wslot_ab(wo_i)[:, ic, oc * 128:(oc + 1) * 128],
                                               AO_fm[:, ic, :], start=(ic == 0), stop=(ic == FC - 1))
                            if ic == FC - 1:
                                em.inc(mm, "s_wb" if last_mat else "s_t")
                    t_wo = em.mark("s_t")
                    wb_wo = em.mark("s_wb")
                    @block.vector
                    def _(vector, ct=ct, oc=oc, bank=bank, t_wo=t_wo, wb_wo=wb_wo, last_mat=last_mat):
                        if last_mat:
                            em.wait(vector, "s_wb", wb_wo)
                        else:
                            em.wait(vector, "s_t", t_wo)
                        em.inc(vector.scalar_tensor_tensor(acc3[:, oc, ct * 512:(ct + 1) * 512], bank[:, 0:512],
                                                           bia[:, 24 + oc:25 + oc], acc3[:, oc, ct * 512:(ct + 1) * 512],
                                                           ALU.add, ALU.add), "s_v")
            wcons[0] = wk_i + 4
            p3_done = em.mark("s_v")

        if UPTO >= 5:
            emit_ffn_supertile(writeback_base=None)


        if UPTO >= 4:
            @block.sync
            def _(sync):
                em.wait(sync, "s_v")
                em.wait(sync, "s_s")
                em.wait(sync, "s_t")
                em.dinc(sync.dma_start(out=y.rearrange("(c p) t -> p c t", p=128), in_=acc3), "s_y")
                em.wait(sync, "s_y")
        if UPTO in (2, 3):
            srcf = x2f if UPTO == 2 else x3f
            @block.gpsimd
            def _(gpsimd):
                em.wait(gpsimd, "s_xa")
                em.wait(gpsimd, "s_xb")
                em.wait(gpsimd, "s_m")
                pid = gpsimd.partition_id()
                off = gpsimd.compute_val((pid % 2) * 2048)
                em.dinc(gpsimd.dma_start(out=y.rearrange("(c p) t -> p c t", p=128),
                                         in_=srcf[:, :, bass.DynSlice(off, 2048)]), "s_y")
                em.wait(gpsimd, "s_y")

    return nc


def stage_inputs(inp):
    g = {k: np.asarray(v) for k, v in inp.items()}
    cores = []
    for c in range(8):
        seq, half = c // 2, c % 2
        m = {
            "xT": np.ascontiguousarray(g["L0_attn_a_x"][seq].T),
            "halfoff": np.array([[half * 2048]], np.int32),
            "basisT": np.ascontiguousarray(g["L0_attn_a_basis"].T),
            "qcT": np.ascontiguousarray(g["L0_attn_a_q_coeffs"].T),
            "kcT": np.ascontiguousarray(g["L0_attn_a_k_coeffs"].T),
            "vcT": np.ascontiguousarray(g["L0_attn_a_v_coeffs"].T),
            "ocT": np.ascontiguousarray(g["L0_attn_a_o_coeffs"].T),
            "scal": np.array([[g["L0_attn_a_decay"], g["L0_attn_a_out_scale"],
                               g["L0_attn_b_temperature"], 0.0]], np.float32),
            "idx": g["L0_attn_b_landmark_idx"].reshape(1, 64).astype(np.int32),
            "WqT": np.ascontiguousarray(g["L0_attn_b_Wq"].T),
            "WkT": np.ascontiguousarray(g["L0_attn_b_Wk"].T),
            "WvT": np.ascontiguousarray(g["L0_attn_b_Wv"].T),
            "WoT": np.ascontiguousarray(g["L0_attn_b_Wo"].T),
            "bqp": np.ascontiguousarray(g["L0_attn_b_bq"].reshape(8, 128).T),
            "bkp": np.ascontiguousarray(g["L0_attn_b_bk"].reshape(8, 128).T),
            "bvp": np.ascontiguousarray(g["L0_attn_b_bv"].reshape(8, 128).T),
            "bop": np.ascontiguousarray(g["L0_attn_b_bo"].reshape(8, 128).T),
            "gate_wT": np.ascontiguousarray(g["L0_ffn_gate_w"].T),
            "sgwT": np.ascontiguousarray(g["L0_ffn_shared_gate_w"].T),
            "w_gate": g["L0_ffn_w_gate"],
            "w_up": g["L0_ffn_w_up"],
            "w_down": g["L0_ffn_w_down"],
            "sw_gate": g["L0_ffn_sw_gate"],
            "sw_up": g["L0_ffn_sw_up"],
            "sw_down": g["L0_ffn_sw_down"],
        }
        cores.append({k: np.ascontiguousarray(v) for k, v in m.items()})
    return cores


def kernel(**inputs):
    in_maps = stage_inputs(inputs)
    nc = build_nc()
    res = run_bass_kernel_spmd(nc, in_maps, core_ids=list(range(8)))
    out = np.empty((4, 4096, 1024), np.float32)
    for c in range(8):
        seq, half = c // 2, c % 2
        yc = res.results[c]["y"]
        out[seq, half * 2048:(half + 1) * 2048, :] = yc.T
    return out


